# version 23
# baseline (speedup 1.0000x reference)
import numpy as np
import concourse.bass as bass
import concourse.mybir as mybir
from concourse.bass_utils import run_bass_kernel_spmd

F32 = mybir.dt.float32
BF16 = mybir.dt.bfloat16
ALU = mybir.AluOpType
AF = mybir.ActivationFunctionType
AX = mybir.AxisListType

B, S, D = 4, 2048, 2048
H, HD = 16, 128
LAT, PD = 512, 64
I = 8192
NCORES = 8
QLEN = 1024
SCALE_ATTN = float(1.0 / np.sqrt(np.float32(HD + PD)))
MAGIC = float(np.float32(3 * 2 ** 22))
LN2C = float(np.log(2.0))
INV_LN2 = float(1.0 / np.log(2.0))

TAPS = []
STOP_AFTER = None


_SLOT_COUNTS = {}


class EQ:

    def __init__(self, key, slot_counts):
        self.key = key
        self.thunks = []
        self.n = 0
        self.waited = {}
        self.slot_counts = slot_counts

    def wait(self, ev):
        if ev is None:
            return
        key, val = ev
        if self.waited.get(key, 0) >= val:
            return
        self.waited[key] = val
        self.thunks.append(("w", key, val))

    def waits(self, evs):
        for e in evs:
            self.wait(e)

    def run(self, fn, sync=False):
        if sync:
            self.n += 1
            self.thunks.append(("i", fn, 1, self.key))
            return (self.key, self.n)
        self.thunks.append(("i", fn, 0, self.key))
        return None

    def dma(self, fn, slot):
        skey = "d_" + slot
        prev = self.slot_counts.get(skey, 0)
        if prev:
            self.wait((skey, prev))
        self.slot_counts[skey] = prev + 16
        self.thunks.append(("i", fn, 16, skey))
        return (skey, self.slot_counts[skey])

    def coll(self, fn, slot):
        skey = "d_" + slot
        prev = self.slot_counts.get(skey, 0)
        if prev:
            self.wait((skey, prev))
        self.slot_counts[skey] = prev + 1
        self.thunks.append(("i", fn, 1, skey))
        return (skey, self.slot_counts[skey])

    def emit(self, eng, sems):
        for t in self.thunks:
            if t[0] == "w":
                eng.wait_ge(sems[t[1]], t[2])
            else:
                ins = t[1](eng)
                if t[2]:
                    ins.then_inc(sems[t[3]], t[2])


class Arena:

    def __init__(self, tensor, total, name):
        self.t = tensor
        self.free = [(0, total, [])]
        self.live = {}
        self.name = name
        self.peak = 0

    def alloc(self, n, tag, align=32, side="left"):
        assert tag not in self.live, tag
        order = list(range(len(self.free)))
        if side == "right":
            order.reverse()
        for i in order:
            off, sz, evs = self.free[i]
            if side == "left":
                aoff = (off + align - 1) // align * align
            else:
                aoff = (off + sz - n) // align * align
                if aoff < off:
                    continue
            if aoff + n <= off + sz:
                repl = [p for p in ((off, aoff - off, list(evs)),
                                    (aoff + n, off + sz - aoff - n, list(evs)))
                        if p[1] > 0]
                self.free[i:i + 1] = repl
                self.live[tag] = (aoff, n)
                used = sum(v[1] for v in self.live.values())
                self.peak = max(self.peak, used)
                return aoff, list(evs)
        raise MemoryError(f"{self.name} OOM for {tag} ({n}); free="
                          f"{[(o, s) for o, s, _ in self.free]} "
                          f"live={ {k: v[1] for k, v in self.live.items()} }")

    def release(self, tag, evs):
        off, n = self.live.pop(tag)
        self.free.append((off, n, list(evs)))
        self.free.sort(key=lambda p: p[0])
        merged = []
        for off, n, evs in self.free:
            if merged and merged[-1][0] + merged[-1][1] == off:
                merged[-1][1] += n
                merged[-1][2] = merged[-1][2] + evs
            else:
                merged.append([off, n, list(evs)])
        self.free = [(o, s, e) for o, s, e in merged]


AR16_N = 80384
AR32_N = 12608


class Ctx:
    def __init__(self, nc, core_id):
        self.nc = nc
        self.core = core_id
        self.half = core_id % 2
        self.q0 = self.half * QLEN
        self.slot_counts = {}
        self.v = EQ("v", self.slot_counts)
        self.a = EQ("a", self.slot_counts)
        self.t = EQ("t", self.slot_counts)
        self.g = EQ("g", self.slot_counts)
        self.sy = EQ("sy", self.slot_counts)
        self.queues = [self.v, self.a, self.t, self.g, self.sy]
        self.ar16_t = nc.alloc_sbuf_tensor("arena16", [128, AR16_N], BF16)
        self.ar32_t = nc.alloc_sbuf_tensor("arena32", [128, AR32_N], F32)
        self.ar16 = Arena(self.ar16_t, AR16_N, "ar16")
        self.ar32 = Arena(self.ar32_t, AR32_N, "ar32")
        self.taps = {}
        self.guards = {}

    def b16(self, n, tag, rows=128, side="left"):
        off, evs = self.ar16.alloc(n, tag, side=side)
        self.guards[tag] = evs
        return self.ar16_t[0:rows, off:off + n]

    def f32(self, n, tag, rows=128):
        off, evs = self.ar32.alloc(n, tag)
        self.guards[tag] = evs
        return self.ar32_t[0:rows, off:off + n]

    def guard(self, *tags):
        out = []
        for tg in tags:
            out.extend(self.guards.get(tg, []))
        return out

    def rel16(self, evs, *tags):
        for tg in tags:
            self.ar16.release(tg, evs)

    def rel32(self, evs, *tags):
        for tg in tags:
            self.ar32.release(tg, evs)


def emit_transpose(cx, src_ap, dst_fn, ntiles, dep_evs, psum):
    t, v = cx.t, cx.v
    t.waits(dep_evs)
    batch = psum.shape[1] // 128
    ev_ev = None
    j = 0
    while j < ntiles:
        k = min(batch, ntiles - j)
        ev = None
        for i in range(k):
            ev = t.run(lambda e, j=j, i=i: e.transpose(
                psum[:, i * 128:(i + 1) * 128],
                src_ap[:, (j + i) * 128:(j + i + 1) * 128], cx.identb),
                sync=(i == k - 1))
        v.wait(ev)
        ev_ev = v.run(lambda e, j=j, k=k: e.tensor_copy(
            dst_fn(j, k), psum[:, 0:k * 128].rearrange("p (k c) -> p k c", c=128)),
            sync=True)
        t.wait(ev_ev)
        j += k
    return ev_ev


def strided_dst(base_ap, tile_stride):
    def f(j, k):
        t = base_ap.tensor
        part = list(base_ap.ap[0])
        return bass.AP(t, base_ap.offset + j * tile_stride,
                       [part, [tile_stride, k], [1, 128]])
    return f


def quant_gen(cx, src_ap, out_ap, W, scr, kind, dep_evs, result,
              transposes=None, inplace_y=False, pool=False):
    v, a = cx.v, cx.a
    nb = W // 32
    amax = scr["amax"][:, :nb]
    pinv = scr["pinv"][:, :nb]
    sc = scr["sc"][:, :nb]
    sb16 = scr["sb16"][:, :nb]
    y = src_ap if inplace_y else scr["y"][:, :W]
    s2 = scr["s2"][:, :W]
    s4 = scr["s4"][:, :W]
    t1 = scr["t1"][:, :W]
    zb = scr["zb"][:, :W]

    v.waits(dep_evs)
    a.waits(dep_evs)
    e_amax = v.run(lambda e: e.tensor_reduce(
        amax.unsqueeze(2), src_ap.rearrange("p (b k) -> p b k", k=32),
        AX.X, ALU.max, apply_absolute_value=True), sync=True)
    yield
    v.wait(e_amax)
    e1 = v.run(lambda e: e.tensor_scalar(amax, amax, 2.0 ** -30, None, ALU.max),
               sync=True)
    yield
    a.wait(e1)
    e2 = a.run(lambda e: e.activation(pinv, amax, AF.Ln), sync=True)
    result["amax_rd"] = e2
    yield
    v.wait(e2)
    e3 = v.run(lambda e: e.tensor_scalar(pinv, pinv, INV_LN2, -0.49609375,
                                         ALU.mult, ALU.add), sync=True)
    yield
    v.wait(e3)
    e4 = v.run(lambda e: e.tensor_scalar(pinv, pinv, MAGIC, MAGIC,
                                         ALU.add, ALU.subtract), sync=True)
    yield
    v.wait(e4)
    e5 = v.run(lambda e: e.tensor_scalar(sc, pinv, LN2C, -2.0 * LN2C,
                                         ALU.mult, ALU.add), sync=True)
    e6 = v.run(lambda e: e.tensor_scalar(pinv, pinv, -LN2C, 2.0 * LN2C,
                                         ALU.mult, ALU.add), sync=True)
    yield
    a.waits([e5, e6])
    e7 = a.run(lambda e: e.activation(sc, sc, AF.Exp), sync=True)
    e8 = a.run(lambda e: e.activation(pinv, pinv, AF.Exp), sync=True)
    yield
    v.waits([e7, e8])
    e9 = v.run(lambda e: e.tensor_copy(sb16, sc), sync=True)
    yield
    e10 = v.run(lambda e: e.tensor_tensor(
        y.rearrange("p (b k) -> p b k", k=32),
        src_ap.rearrange("p (b k) -> p b k", k=32),
        pinv.unsqueeze(2).broadcast_to([128, nb, 32]), ALU.mult), sync=True)
    if not inplace_y:
        result["yused"] = e10
    yield
    a.wait(e10)
    ep2 = a.run(lambda e: e.activation(s2, y, AF.Sign, bias=cx.cneg2), sync=True)
    en2 = a.run(lambda e: e.activation(t1, y, AF.Sign, scale=-1.0,
                                       bias=cx.cneg2), sync=True)
    ep4 = a.run(lambda e: e.activation(s4, y, AF.Sign, bias=cx.cneg4), sync=True)
    en4 = a.run(lambda e: e.activation(zb, y, AF.Sign, scale=-1.0,
                                       bias=cx.cneg4), sync=True)
    yield
    gq = cx.g if pool else v
    gq.waits([ep2, en2, ep4, en4])
    ga = gq.run(lambda e: e.tensor_tensor(s2, s2, t1, ALU.add), sync=True)
    gb = gq.run(lambda e: e.tensor_tensor(s4, s4, zb, ALU.add), sync=True)
    yield
    v.waits([ga, gb])
    if kind == "act":
        e11 = v.run(lambda e: e.scalar_tensor_tensor(
            t1, s2, -2.0, s4, ALU.mult, ALU.subtract), sync=True)
        cadd, cm2, cm0, cm4, clip = 2.0, 0.0625, 0.5, 0.125, 7.5
    else:
        e4q = v.run(lambda e: e.tensor_scalar(t1, s4, 0.25, None, ALU.mult),
                    sync=True)
        yield
        v.wait(e4q)
        e11 = v.run(lambda e: e.scalar_tensor_tensor(
            t1, s2, -0.5, t1, ALU.mult, ALU.subtract), sync=True)
        cadd, cm2, cm0, cm4, clip = 0.5, 0.25, 2.0, 0.5, 6.0
    yield
    v.wait(e11)
    e12 = v.run(lambda e: e.scalar_tensor_tensor(
        y, t1, cadd, y, ALU.add, ALU.mult), sync=True)
    yield
    v.wait(e12)
    e13 = v.run(lambda e: e.tensor_scalar(zb, y, MAGIC, MAGIC,
                                          ALU.add, ALU.subtract), sync=True)
    if inplace_y:
        result["yused"] = e13
    e14 = v.run(lambda e: e.tensor_scalar(t1, s2, cm2, cm0, ALU.mult, ALU.add),
                sync=True)
    yield
    v.waits([e13, e14])
    e15 = v.run(lambda e: e.scalar_tensor_tensor(
        t1, s4, cm4, t1, ALU.mult, ALU.add), sync=True)
    yield
    gq.waits([e15] + ([e13] if pool else []))
    e16 = gq.run(lambda e: e.tensor_tensor(zb, zb, t1, ALU.mult), sync=True)
    yield
    v.wait(e16)
    e17 = v.run(lambda e: e.tensor_scalar(zb, zb, clip, -clip,
                                          ALU.min, ALU.max), sync=True)
    yield
    v.waits([e17, e9])
    e18 = v.run(lambda e: e.tensor_tensor(
        out_ap.rearrange("p (b k) -> p b k", k=32),
        zb.rearrange("p (b k) -> p b k", k=32),
        sb16.unsqueeze(2).broadcast_to([128, nb, 32]), ALU.mult), sync=True)
    result["done"] = e18
    if transposes:
        for (tsrc, tdst, tn, tpsum) in transposes:
            result["tdone"] = emit_transpose(cx, tsrc, tdst, tn, [e18], tpsum)


class QuantPipe:

    def __init__(self, depth=2):
        self.depth = depth
        self.active = []

    def push(self, gen):
        self.active.append(gen)
        while len(self.active) >= self.depth:
            self.step_until_done_one()

    def step_until_done_one(self):
        while self.active:
            for g in list(self.active):
                try:
                    next(g)
                except StopIteration:
                    self.active.remove(g)
                    return

    def flush(self):
        while self.active:
            self.step_until_done_one()


def emit_rope(cx, psrc, dst, cosap, sinap, W, ev_dep):
    v = cx.v
    r1 = cx.f32(W, "ropeA", rows=32)
    r2 = cx.f32(W, "ropeB", rows=32)
    v.waits(cx.guard("ropeA", "ropeB"))
    v.waits(ev_dep if isinstance(ev_dep, list) else [ev_dep])
    lo, hi = psrc[0:32, :], psrc[32:64, :]
    ev = v.run(lambda e: e.tensor_tensor(r1, lo, cosap, ALU.mult), sync=True)
    ev = v.run(lambda e: e.tensor_tensor(r2, hi, sinap, ALU.mult), sync=True)
    v.wait(ev)
    ev = v.run(lambda e: e.tensor_tensor(dst[0:32, :], r1, r2, ALU.subtract),
               sync=True)
    v.wait(ev)
    ev = v.run(lambda e: e.tensor_tensor(r1, hi, cosap, ALU.mult), sync=True)
    ev = v.run(lambda e: e.tensor_tensor(r2, lo, sinap, ALU.mult), sync=True)
    v.wait(ev)
    ev = v.run(lambda e: e.tensor_tensor(dst[32:64, :], r1, r2, ALU.add),
               sync=True)
    cx.rel32([ev], "ropeA", "ropeB")
    return ev


def emit_ln(cx, buf, out_ap, base, dep_evs):
    v, a = cx.v, cx.a
    s24 = cx.statst[:, base:base + 24]
    mean = cx.statst[:, base + 24:base + 25]
    var = cx.statst[:, base + 25:base + 26]
    rstd = cx.statst[:, base + 26:base + 27]
    v.waits(dep_evs)
    ev = None
    for j in range(4):
        ev = v.run(lambda e, j=j: e.bn_stats(
            s24[:, 6 * j:6 * j + 6], buf[:, 512 * j:512 * (j + 1)]),
            sync=(j == 3))
    v.wait(ev)
    e2 = v.run(lambda e: e.bn_aggr(cx.statst[:, base + 24:base + 26], s24),
               sync=True)
    a.wait(e2)
    e3 = a.run(lambda e: e.activation(rstd, var, AF.Ln, bias=cx.ceps), sync=True)
    a.wait(e3)
    e4 = a.run(lambda e: e.activation(rstd, rstd, AF.Exp, scale=-0.5), sync=True)
    v.wait(e4)
    e5 = v.run(lambda e: e.tensor_scalar(out_ap, buf, mean, rstd,
                                         ALU.subtract, ALU.mult), sync=True)
    return e5


def emit_quant(cx, src_ap, out_ap, W, scr, kind, dep_evs, inplace_y=False,
               transposes=None, result=None, pool=False):
    if result is None:
        result = {}
    for _ in quant_gen(cx, src_ap, out_ap, W, scr, kind, dep_evs, result,
                       transposes=transposes, inplace_y=inplace_y, pool=pool):
        pass
    return result["tdone"] if transposes else result["done"]


def build_core(core_id):
    nc = bass.Bass("TRN2", target_bir_lowering=False)
    cx = Ctx(nc, core_id)
    v, a, t, g, sy = cx.v, cx.a, cx.t, cx.g, cx.sy
    q0 = cx.q0

    xseq = nc.dram_tensor("xseq", [S, D], F32, kind="ExternalInput")
    xown = nc.dram_tensor("xown", [QLEN, D], F32, kind="ExternalInput")
    wdkv = nc.dram_tensor("W_DKV", [D, LAT], F32, kind="ExternalInput")
    wdq = nc.dram_tensor("W_DQ", [D, LAT], F32, kind="ExternalInput")
    wuk = nc.dram_tensor("W_UK", [LAT, D], F32, kind="ExternalInput")
    wuv = nc.dram_tensor("W_UV", [LAT, D], F32, kind="ExternalInput")
    wuq = nc.dram_tensor("W_UQ", [LAT, D], F32, kind="ExternalInput")
    wkr = nc.dram_tensor("W_KR", [D, PD], F32, kind="ExternalInput")
    wqr = nc.dram_tensor("W_QR", [LAT, H * PD], F32, kind="ExternalInput")
    wo = nc.dram_tensor("W_O", [D, D], F32, kind="ExternalInput")
    wg_sl = nc.dram_tensor("wg_sl", [I // 8, D], F32, kind="ExternalInput")
    wu_sl = nc.dram_tensor("wu_sl", [I // 8, D], F32, kind="ExternalInput")
    wd_sl = nc.dram_tensor("wd_sl", [D // 8, I], F32, kind="ExternalInput")
    ropec = nc.dram_tensor("rope_cos", [32, S], F32, kind="ExternalInput")
    ropes = nc.dram_tensor("rope_sin", [32, S], F32, kind="ExternalInput")
    ropecq = nc.dram_tensor("rope_cos_q", [32, QLEN], F32, kind="ExternalInput")
    ropesq = nc.dram_tensor("rope_sin_q", [32, QLEN], F32, kind="ExternalInput")
    identd = nc.dram_tensor("identd", [128, 128], F32, kind="ExternalInput")
    constsd = nc.dram_tensor("constsd", [128, 8], F32, kind="ExternalInput")
    out = nc.dram_tensor("out", [QLEN, D], F32, kind="ExternalOutput")
    y1d = nc.dram_tensor("y1scr", [QLEN, D], F32, kind="Internal")
    wgT_c = nc.dram_tensor("wgT_c", [32, 128, 4096], BF16, kind="Internal")
    wuT_c = nc.dram_tensor("wuT_c", [32, 128, 4096], BF16, kind="Internal")
    wdT_c = nc.dram_tensor("wdT_c", [16, 128, 8192], BF16, kind="Internal")
    cg_c = nc.dram_tensor("cg_c", [4, 128, 4096], BF16, kind="Internal")
    cu_c = nc.dram_tensor("cu_c", [4, 128, 4096], BF16, kind="Internal")
    cd_c = nc.dram_tensor("cd_c", [2, 128, 8192], BF16, kind="Internal")

    consts = nc.alloc_sbuf_tensor("consts", [128, 8], F32)
    identf = nc.alloc_sbuf_tensor("identf", [128, 128], F32)
    identb = nc.alloc_sbuf_tensor("identb", [128, 128], BF16)
    stats = nc.alloc_sbuf_tensor("stats", [128, 84], F32)
    cx.statst = stats
    cx.identb = identb[:, :]
    cx.cneg2 = consts[:, 0:1]
    cx.cneg4 = consts[:, 1:2]
    cx.ceps = consts[:, 2:3]
    nc.const_aps.register(F32, -2.0, consts[:, 0:1])
    nc.const_aps.register(F32, -4.0, consts[:, 1:2])
    nc.const_aps.register(F32, 1e-5, consts[:, 2:3])

    psa = nc.alloc_psum_tensor("psa", [128, 2048], F32)
    psb = nc.alloc_psum_tensor("psb", [128, 1024], F32)
    pta = nc.alloc_psum_tensor("pta", [128, 1024], BF16)
    ptb = nc.alloc_psum_tensor("ptb", [128, 1024], BF16)

    e_consts = sy.dma(lambda e: e.dma_start(consts[:, :], constsd[:, :]), "ini0")
    e_identf = sy.dma(lambda e: e.dma_start(identf[:, :], identd[:, :]), "ini1")
    cx.e_cosq, cx.e_sinq = None, None
    v.wait(e_identf)
    e_identb = v.run(lambda e: e.tensor_copy(identb[:, :], identf[:, :]), sync=True)
    t.wait(e_identb)
    a.wait(e_consts)
    v.wait(e_consts)

    csb = cx.f32(S, "cos", rows=64)
    cosst = csb[0:32, :]
    sinst = bass.AP(csb.tensor, csb.offset + 32 * AR32_N, list(csb.ap)) \
        if False else None
    sy.waits(cx.guard("cos"))
    e_cos = sy.dma(lambda e: e.dma_start(cosst, ropec[:, :]), "ini2")
    sinst = cx.ar32_t[32:64, cx.ar32.live["cos"][0]:cx.ar32.live["cos"][0] + S]
    e_sin = sy.dma(lambda e: e.dma_start(sinst, ropes[:, :]), "ini3")
    cx.e_cos, cx.e_sin = e_cos, e_sin
    CKVT = cx.b16(4 * 2048, "CKVT")
    KRT = cx.b16(2048, "KRT", rows=64)

    def load_w_contract_major(dram, rows, cols, tag):
        nk = rows // 128
        bf = cx.b16(nk * cols, tag)
        g.waits(cx.guard(tag))
        ev = g.dma(lambda e, bf=bf, dram=dram, cols=cols: e.dma_start(
            bf.rearrange("p (c l) -> p c l", l=cols),
            dram[:, :].rearrange("(c p) l -> p c l", p=128)), "wl_" + tag)
        return bf, ev

    WDKVb, e_wdkv = load_w_contract_major(wdkv, D, LAT, "WDKVb")
    WKRb, e_wkr = load_w_contract_major(wkr, D, PD, "WKRb")
    e_ckvt_ev = [None] * 4
    e_cqt_ev = [None] * 2
    e_krt_done = [None]

    def emit_B_ckvt_tt(tt):
        t.waits([e_wdkv] + [e_xt_tc[4 * tt + i] for i in range(4)])
        for lc in range(4):
            t.wait(e_ckvt_ev[lc])
            ev = None
            for kc in range(16):
                ev = t.run(lambda e, lc=lc, kc=kc, tt=tt: e.matmul(
                    psa[:, lc * 512:(lc + 1) * 512],
                    WDKVb[:, kc * LAT + lc * 128: kc * LAT + (lc + 1) * 128],
                    XTr[tt % 2][:, kc * 512:(kc + 1) * 512],
                    start=(kc == 0), stop=(kc == 15)),
                    sync=(kc == 15))
            a.wait(ev)
            e_ev = a.run(lambda e, lc=lc, tt=tt: e.activation(
                CKVT[:, lc * 2048 + tt * 512: lc * 2048 + (tt + 1) * 512],
                psa[:, lc * 512:(lc + 1) * 512], AF.Copy), sync=True)
            e_ckvt_ev[lc] = e_ev

    def emit_B_kr_tt(tt):
        t.waits([e_wkr] + [e_xt_tc[4 * tt + i] for i in range(4)])
        t.wait(e_krt_done[0])
        ev = None
        for kc in range(16):
            ev = t.run(lambda e, kc=kc, tt=tt: e.matmul(
                psb[0:64, 0:512],
                WKRb[:, kc * PD:(kc + 1) * PD],
                XTr[tt % 2][:, kc * 512:(kc + 1) * 512],
                start=(kc == 0), stop=(kc == 15)),
                sync=(kc == 15))
        e_krt_done[0] = emit_rope(cx, psb[0:64, 0:512],
                                  KRT[:, tt * 512:(tt + 1) * 512],
                                  cosst[0:32, tt * 512:(tt + 1) * 512],
                                  sinst[0:32, tt * 512:(tt + 1) * 512], 512,
                                  [ev, cx.e_cos, cx.e_sin])

    def emit_B_cqt_tt(qt):
        t.waits([e_wdq] + [e_xtq_tc[4 * qt + i] for i in range(4)])
        t.wait(e_krt_done[0])
        for lc2 in range(2):
            for lc in (2 * lc2, 2 * lc2 + 1):
                t.wait(e_cqt_ev[lc % 2])
                ev = None
                for kc in range(16):
                    ev = t.run(lambda e, lc=lc, kc=kc, qt=qt: e.matmul(
                        psb[:, (lc % 2) * 512:((lc % 2) + 1) * 512],
                        WDQb[:, kc * LAT + lc * 128: kc * LAT + (lc + 1) * 128],
                        XTQ[:, kc * 1024 + qt * 512: kc * 1024 + (qt + 1) * 512],
                        start=(kc == 0), stop=(kc == 15)),
                        sync=(kc == 15))
                a.wait(ev)
                e_ev = a.run(lambda e, lc=lc, qt=qt: e.activation(
                    CQT[:, lc * 1024 + qt * 512: lc * 1024 + (qt + 1) * 512],
                    psb[:, (lc % 2) * 512:((lc % 2) + 1) * 512], AF.Copy),
                    sync=True)
                e_cqt_ev[lc % 2] = e_ev

    XTr = [cx.b16(16 * 512, "XTr0"), cx.b16(16 * 512, "XTr1")]
    e_xt_tc = [None] * 16
    e_xtq_tc = [None] * 8
    xch = [cx.f32(2048, "xA"), cx.f32(2048, "xB")]
    lnout = cx.b16(2048, "lnout")
    e_load = [None, None]
    e_app_prev = None
    e_sq_last = None
    sy.waits(cx.guard("xA", "xB"))

    e_load[0] = sy.dma(lambda e: e.dma_start(xch[0], xseq[0:128, :]), "xb0")
    e_load[1] = sy.dma(lambda e: e.dma_start(xch[1], xseq[128:256, :]), "xb1")
    for tc in range(16):
        buf = xch[tc % 2]
        par = tc % 2
        deps = [e_load[par]]
        if e_app_prev is not None:
            deps.append(e_app_prev)
        e_app = emit_ln(cx, buf, lnout, 27 * par, deps)
        e_app_prev = e_app
        if tc + 2 < 16:
            sy.wait(e_app)
            e_load[par] = sy.dma(lambda e, tc=tc, buf=buf: e.dma_start(
                buf, xseq[(tc + 2) * 128:(tc + 3) * 128, :]), "xb%d" % par)
        e_xt_tc[tc] = emit_transpose(
            cx, lnout,
            strided_dst(XTr[(tc // 4) % 2][:, (tc % 4) * 128:(tc % 4) * 128 + 128],
                        512),
            16, [e_app], pta if par == 0 else ptb)
        v.wait((cx.t.key, cx.t.n))
        if tc % 4 == 3:
            emit_B_ckvt_tt(tc // 4)
            emit_B_kr_tt(tc // 4)
    XTQ = cx.b16(16 * 1024, "XTQ")
    CQT = cx.b16(4 * 1024, "CQT")
    WDQb, e_wdq = load_w_contract_major(wdq, D, LAT, "WDQb")
    v.waits(cx.guard("XTQ"))
    sy.waits([e_app_prev])
    e_load[0] = sy.dma(lambda e: e.dma_start(xch[0], xown[0:128, :]), "xb0")
    e_load[1] = sy.dma(lambda e: e.dma_start(xch[1], xown[128:256, :]), "xb1")
    for tc in range(8):
        buf = xch[tc % 2]
        par = tc % 2
        e_app = emit_ln(cx, buf, lnout, 27 * par, [e_load[par], e_app_prev])
        e_app_prev = e_app
        if tc + 2 < 8:
            sy.waits([e_app])
            e_load[par] = sy.dma(lambda e, tc=tc, buf=buf: e.dma_start(
                buf, xown[(tc + 2) * 128:(tc + 3) * 128, :]), "xb%d" % par)
        e_xtq_tc[tc] = emit_transpose(
            cx, lnout, strided_dst(XTQ[:, tc * 128: tc * 128 + 128], 1024),
            16, [e_app], pta if par == 0 else ptb)
        v.wait((cx.t.key, cx.t.n))
        if tc % 4 == 3:
            emit_B_cqt_tt(tc // 4)

    e_XT = (cx.v.key, cx.v.n)
    e_trans_last = (cx.t.key, cx.t.n)

    cx.rel32([e_app_prev], "xA", "xB")
    cx.rel16([(cx.t.key, cx.t.n)], "lnout", "WDKVb", "WDQb", "WKRb", "XTQ",
             "XTr0", "XTr1")
    if STOP_AFTER == "B":
        _finalize(cx, nc, out)
        return nc, cx


    raww2 = [cx.f32(1024, "rawwA"), cx.f32(1024, "rawwB")]
    wbuf2 = [cx.b16(1024, "wbufA"), cx.b16(1024, "wbufB")]
    scrW = {
        "amax": cx.f32(32, "qw_amax"), "pinv": cx.f32(32, "qw_pinv"),
        "sc": cx.f32(32, "qw_sc"),
        "s2": cx.b16(1024, "qw_s2"), "s4": cx.b16(1024, "qw_s4"),
        "t1": cx.b16(1024, "qw_t1"), "zb": cx.b16(1024, "qw_zb"),
        "sb16": cx.b16(32, "qw_sb16"), "wqb": cx.b16(1024, "qw_wqb"),
    }
    sy.waits(cx.guard("rawwA", "rawwB"))
    for q_ in (v, a):
        q_.waits(cx.guard("wbufA", "wbufB", "qw_s2", "qw_s4", "qw_t1", "qw_zb",
                          "qw_sb16", "qw_wqb", "qw_amax", "qw_pinv", "qw_sc"))

    wjob_defs = []
    for igl in range(4):
        for ch in range(2):
            for cc in range(2):
                wjob_defs.append((wg_sl[igl * 256 + ch * 128:
                                        igl * 256 + (ch + 1) * 128,
                                        cc * 1024:(cc + 1) * 1024],
                                  cg_c, igl, cc * 2048 + ch * 128))
    for igl in range(4):
        for ch in range(2):
            for cc in range(2):
                wjob_defs.append((wu_sl[igl * 256 + ch * 128:
                                        igl * 256 + (ch + 1) * 128,
                                        cc * 1024:(cc + 1) * 1024],
                                  cu_c, igl, cc * 2048 + ch * 128))
    for ch in range(2):
        for cc in range(8):
            wjob_defs.append((wd_sl[ch * 128:(ch + 1) * 128,
                                    cc * 1024:(cc + 1) * 1024],
                              cd_c, cc // 4, ((cc % 4) * 8) * 256 + ch * 128))
    wres = [None] * len(wjob_defs)
    e_wld = [None] * len(wjob_defs)
    e_cw = [None, None]
    gath = {}
    gather_at = {15: ("agg", cg_c, wgT_c), 31: ("agu", cu_c, wuT_c),
                 47: ("agd", cd_c, wdT_c)}

    def wjob_load(k):
        dslice = wjob_defs[k][0]
        par = k % 2
        if k >= 2:
            sy.waits([wres[k - 2].get("yused")])
        e_wld[k] = sy.dma(lambda e, dslice=dslice, par=par: e.dma_start(
            raww2[par], dslice), "wld" + ("A" if par == 0 else "B"))

    def wjob_gen(k):
        if k + 1 < len(wjob_defs):
            wjob_load(k + 1)
        _, cten, idx, coff = wjob_defs[k]
        par = k % 2
        res = {}
        wres[k] = res
        trans = [(scrW["wqb"][:, 0:1024],
                  strided_dst(wbuf2[par][:, 0:128], 128), 8,
                  pta if par == 0 else ptb)]
        deps = [e_wld[k], e_cw[par]]
        for _x in quant_gen(cx, raww2[par], scrW["wqb"][:, 0:1024], 1024,
                            scrW, "w", deps, res, transposes=trans,
                            inplace_y=True):
            yield
        g.waits([res["tdone"]])
        e_cw[par] = g.dma(
            lambda e, cten=cten, idx=idx, coff=coff, par=par: e.dma_start(
                bass.AP(cten, idx * 128 * cten.shape[2] + coff,
                        [[cten.shape[2], 128], [256, 8], [1, 128]]),
                wbuf2[par][:, 0:1024].rearrange("p (j c) -> p j c", c=128)),
            "cw" + ("A" if par == 0 else "B"))
        if k in gather_at:
            slot, cten_g, gten = gather_at[k]
            g.waits([e_cw[0], e_cw[1]])
            gath[slot] = g.coll(
                lambda e, cten_g=cten_g, gten=gten: e.collective_compute(
                    "AllGather", ALU.bypass,
                    replica_groups=[list(range(NCORES))],
                    ins=[cten_g.ap().opt()], outs=[gten.ap().opt()]), slot)

    wpipe = {"cur": None, "next": 0}

    def pump_w(nsteps):
        while nsteps > 0:
            if wpipe["cur"] is None:
                k = wpipe["next"]
                if k >= len(wjob_defs):
                    return
                if k == 0:
                    wjob_load(0)
                wpipe["next"] = k + 1
                wpipe["cur"] = wjob_gen(k)
            try:
                next(wpipe["cur"])
                nsteps -= 1
            except StopIteration:
                wpipe["cur"] = None

    def finish_wjobs():
        pump_w(1 << 30)
        ev1, ev2, ev3 = gath["agg"], gath["agu"], gath["agd"]
        e_done = (v.key, v.n)
        cx.rel32([wres[-1].get("yused"), wres[-2].get("yused")],
                 "rawwA", "rawwB")
        cx.rel32([e_done], "qw_amax", "qw_pinv", "qw_sc")
        cx.rel16([e_cw[0], e_cw[1]], "wbufA", "wbufB")
        cx.rel16([e_done], "qw_s2", "qw_s4", "qw_t1", "qw_zb", "qw_sb16",
                 "qw_wqb")
        return ev1, ev2, ev3

    t.waits(e_ckvt_ev + e_cqt_ev + [e_krt_done[0]])
    cosq = cx.f32(QLEN, "cosq", rows=32)
    sinq = cx.ar32_t[32:64, cx.ar32.live["cosq"][0]:cx.ar32.live["cosq"][0] + QLEN]
    sy.waits(cx.guard("cosq"))
    e_cosq = sy.dma(lambda e: e.dma_start(cosq, ropecq[:, :]), "ini4")
    e_sinq = sy.dma(lambda e: e.dma_start(sinq, ropesq[:, :]), "ini5")
    cx.e_cosq, cx.e_sinq = e_cosq, e_sinq
    WUQg = [cx.b16(4 * 256, "WUQg0"), cx.b16(4 * 256, "WUQg1")]
    WQRg = [cx.b16(4 * 128, "WQRg0"), cx.b16(4 * 128, "WQRg1")]
    WUKg = [cx.b16(4 * 256, "WUKg0"), cx.b16(4 * 256, "WUKg1")]
    WUVg = [cx.b16(4 * 256, "WUVg0"), cx.b16(4 * 256, "WUVg1")]
    g.waits(cx.guard("WUQg0", "WUQg1", "WQRg0", "WQRg1", "WUKg0", "WUKg1",
                     "WUVg0", "WUVg1"))
    e_gw = [None, None]
    e_genmm = [None, None]

    def load_group_w(grp):
        par = grp % 2
        g.waits([e_genmm[par]])
        evs = []
        for (dst, wsrc, colw) in ((WUQg[par], wuq, 128), (WQRg[par], wqr, 64),
                                  (WUKg[par], wuk, 128), (WUVg[par], wuv, 128)):
            evs.append(g.dma(lambda e, dst=dst, wsrc=wsrc, colw=colw, grp=grp:
                             e.dma_start(
                dst.rearrange("p (c l) -> p c l", l=2 * colw),
                wsrc[:, grp * 2 * colw:(grp + 1) * 2 * colw]
                .rearrange("(c p) l -> p c l", p=128)), "gw%d" % par))
        e_gw[par] = evs

    OT = cx.b16(16 * 1024, "OT")
    QCg = cx.b16(2 * 1024, "QCg")
    QRTg = cx.b16(2 * 1024, "QRTg", rows=64)
    KCg = cx.b16(2 * 2048, "KCg")
    Vg = cx.b16(16 * 256, "Vg")
    Pb2 = [cx.b16(2048, "Pb0"), cx.b16(2048, "Pb1")]
    PT4 = cx.b16(16 * 256, "PT4")

    v.waits(cx.guard("OT", "QCg", "QRTg", "KCg", "Vg", "Pb0", "Pb1", "PT4"))
    a.waits(cx.guard("Pb0", "Pb1", "OT", "QCg", "KCg", "Vg"))
    e_exp_hist = [None, None]
    e_ps_hist = [None, None]
    e_pt_hist = [None, None]
    e_exptt_prev = [None] * 4
    e_pv = None
    load_group_w(0)
    for grp in range(8):
        h0 = 2 * grp
        par_g = grp % 2
        if grp + 1 < 8:
            load_group_w(grp + 1)
        t.waits(e_gw[par_g])

        def emit_qc(hh):
            ev_mm = None
            for kc in range(4):
                for qt in range(2):
                    ev_mm = t.run(lambda e, hh=hh, kc=kc, qt=qt,
                                  par_g=par_g: e.matmul(
                        psb[:, qt * 512:(qt + 1) * 512],
                        WUQg[par_g][:, kc * 256 + hh * 128:
                                    kc * 256 + hh * 128 + 128],
                        CQT[:, kc * 1024 + qt * 512: kc * 1024 + (qt + 1) * 512],
                        start=(kc == 0), stop=(kc == 3)),
                        sync=(kc == 3 and qt == 1))
            a.wait(ev_mm)
            return a.run(lambda e, hh=hh: e.activation(
                QCg[:, hh * 1024:(hh + 1) * 1024], psb[:, :], AF.Copy,
                scale=SCALE_ATTN), sync=True)

        def emit_kc(hh):
            ev_mm = None
            for kc in range(4):
                for tt in range(4):
                    ev_mm = t.run(lambda e, hh=hh, kc=kc, tt=tt,
                                  par_g=par_g: e.matmul(
                        psa[:, tt * 512:(tt + 1) * 512],
                        WUKg[par_g][:, kc * 256 + hh * 128:
                                    kc * 256 + hh * 128 + 128],
                        CKVT[:, kc * 2048 + tt * 512: kc * 2048 + (tt + 1) * 512],
                        start=(kc == 0), stop=(kc == 3)),
                        sync=(kc == 3 and tt == 3))
            a.wait(ev_mm)
            return a.run(lambda e, hh=hh: e.activation(
                KCg[:, hh * 2048:(hh + 1) * 2048], psa[:, :], AF.Copy),
                sync=True)

        def emit_qrt(hh):
            ev_mm = None
            for kc in range(4):
                for qt in range(2):
                    ev_mm = t.run(lambda e, hh=hh, kc=kc, qt=qt,
                                  par_g=par_g: e.matmul(
                        psb[0:64, qt * 512:(qt + 1) * 512],
                        WQRg[par_g][:, kc * 128 + hh * 64:
                                    kc * 128 + hh * 64 + 64],
                        CQT[:, kc * 1024 + qt * 512: kc * 1024 + (qt + 1) * 512],
                        start=(kc == 0), stop=(kc == 3)),
                        sync=(kc == 3 and qt == 1))
            e_rope = emit_rope(cx, psb[0:64, 0:1024],
                               QRTg[:, hh * 1024:(hh + 1) * 1024],
                               cosq, sinq, 1024,
                               [ev_mm, cx.e_cosq, cx.e_sinq])
            return ev_mm, e_rope

        e_qc0 = emit_qc(0)
        e_kc0 = emit_kc(0)
        pump_w(12)
        t.wait(e_qc0)
        e_qc1 = emit_qc(1)
        t.wait(e_kc0)
        e_kc1 = emit_kc(1)
        pump_w(12)
        t.wait(e_qc1)
        for tcg in range(4):
            ev_mm = None
            for tc4 in range(4):
                tc = tcg * 4 + tc4
                for lc in range(4):
                    ev_mm = t.run(lambda e, tc=tc, tc4=tc4, lc=lc,
                                  par_g=par_g: e.matmul(
                        psb[:, tc4 * 256:(tc4 + 1) * 256],
                        CKVT[:, lc * 2048 + tc * 128: lc * 2048 + tc * 128 + 128],
                        WUVg[par_g][:, lc * 256: lc * 256 + 256],
                        start=(lc == 0), stop=(lc == 3)),
                        sync=(tc4 == 3 and lc == 3))
            a.wait(ev_mm)
            e_vg = a.run(lambda e, tcg=tcg: e.activation(
                Vg[:, tcg * 1024:(tcg + 1) * 1024], psb[:, :], AF.Copy), sync=True)
            t.wait(e_vg)
        ev_qr0, e_rope0 = emit_qrt(0)
        t.wait(e_rope0)
        ev_qr1, e_rope1 = emit_qrt(1)
        e_genmm[par_g] = ev_qr1
        t.wait(e_kc1)
        pump_w(8)
        pend_pt = None
        for u in range(17):
            if True:
                hh = min(u // 8, 1)
                h = h0 + hh
                qc = u % 8
                if u < 16:
                    par = qc % 2
                    pm = stats[:, 54 + 8 * par: 58 + 8 * par]
                    vmax = stats[:, 58 + 8 * par: 59 + 8 * par]
                    nm = stats[:, 59 + 8 * par: 60 + 8 * par]
                    ssum = stats[:, 60 + 8 * par: 61 + 8 * par]
                    rr = stats[:, 61 + 8 * par: 62 + 8 * par]
                    Pb = Pb2[par]
                    ev_tt = []
                    for tt in range(4):
                        t.wait(e_exptt_prev[tt])
                        t.run(lambda e, hh=hh, qc=qc, tt=tt, h0=h0: e.matmul(
                            psa[:, tt * 512:(tt + 1) * 512],
                            QCg[:, hh * 1024 + qc * 128: hh * 1024 + qc * 128 + 128],
                            KCg[:, hh * 2048 + tt * 512: hh * 2048 + (tt + 1) * 512],
                            start=True, stop=False))
                    for tt in range(4):
                        ev_tt.append(t.run(lambda e, hh=hh, qc=qc, tt=tt, h0=h0: e.matmul(
                            psa[:, tt * 512:(tt + 1) * 512],
                            QRTg[:, hh * 1024 + qc * 128: hh * 1024 + qc * 128 + 128],
                            KRT[:, tt * 512:(tt + 1) * 512],
                            start=False, stop=True), sync=True))
                    v.wait(e_exp_hist[par])
                    for tt in range(4):
                        v.wait(ev_tt[tt])
                        v.run(lambda e, tt=tt, pm=pm: e.tensor_reduce(
                            pm[:, tt:tt + 1], psa[:, tt * 512:(tt + 1) * 512],
                            AX.X, ALU.max), sync=(tt == 3))
                    e_pm = (v.key, v.n)
                    a.wait(e_pt_hist[par])
                    v.wait(e_pm)
                    e_mx = v.run(lambda e, pm=pm, vmax=vmax: e.tensor_reduce(
                        vmax, pm, AX.X, ALU.max, negate=True), sync=True)
                    a.wait(e_mx)
                    for tt in range(4):
                        e_exptt_prev[tt] = a.run(
                            lambda e, tt=tt, vmax=vmax, pm=pm, Pb=Pb: e.activation(
                                Pb[:, tt * 512:(tt + 1) * 512],
                                psa[:, tt * 512:(tt + 1) * 512], AF.Exp,
                                bias=vmax,
                                accum_out=pm[:, tt:tt + 1]), sync=True)
                    e_exp = e_exptt_prev[3]
                    e_exp_hist[par] = e_exp
                if pend_pt is not None:
                    phh, pqc = pend_pt
                    ph = h0 + phh
                    if pqc % 2 == 0 and e_pv is not None:
                        v.wait(e_pv)
                    pp = pqc % 2
                    emit_transpose(
                        cx, Pb2[pp][:, 0:1024],
                        strided_dst(PT4[:, pp * 128: pp * 128 + 128], 256),
                        8, [e_ps_hist[pp]], pta)
                    emit_transpose(
                        cx, Pb2[pp][:, 1024:2048],
                        strided_dst(PT4[:, pp * 128 + 2048: pp * 128 + 2176],
                                    256),
                        8, [e_ps_hist[pp]], ptb)
                    e_pt_hist[pp] = (t.key, t.n)
                    if pqc % 2 == 1:
                        if pqc == 1 and phh == 0:
                            t.wait(e_rope1)
                        ev_mm = None
                        for tc in range(16):
                            ev_mm = t.run(lambda e, hh=phh, tc=tc, h0=h0: e.matmul(
                                psb[:, 0:256],
                                Vg[:, tc * 256 + hh * 128: tc * 256 + hh * 128 + 128],
                                PT4[:, tc * 256:(tc + 1) * 256],
                                start=(tc == 0), stop=(tc == 15)),
                                sync=(tc == 15))
                        e_pv = ev_mm
                        a.wait(ev_mm)
                        e_ot = a.run(lambda e, h=ph, pqc=pqc: e.activation(
                            OT[:, h * 1024 + (pqc // 2) * 256:
                               h * 1024 + (pqc // 2) * 256 + 256],
                            psb[:, 0:256], AF.Copy), sync=True)
                        t.wait(e_ot)
                    pend_pt = None
                if u == 16:
                    break
                pump_w(5)
                v.wait(e_exp)
                e_sm = v.run(lambda e, pm=pm, ssum=ssum: e.tensor_reduce(
                    ssum, pm, AX.X, ALU.add), sync=True)
                v.wait(e_sm)
                e_rr = v.run(lambda e, ssum=ssum, rr=rr: e.reciprocal(rr, ssum),
                             sync=True)
                v.wait(e_rr)
                e_ps = v.run(lambda e, rr=rr, Pb=Pb: e.tensor_scalar(
                    Pb, Pb, rr, None, ALU.mult), sync=True)
                e_ps_hist[par] = e_ps
                pend_pt = (hh, qc)
    e_agg, e_agu, e_agd = finish_wjobs()
    cx.taps["OT"] = (OT, [128, 16 * 1024])
    e_tC = (t.key, t.n)
    cx.rel16([e_tC], "QCg", "QRTg", "KCg", "Vg", "Pb0", "Pb1", "PT4",
             "WUQg0", "WUQg1", "WQRg0", "WQRg1", "WUKg0", "WUKg1",
             "WUVg0", "WUVg1", "CKVT", "CQT", "KRT")
    cx.rel32([e_tC], "cos", "cosq")

    if STOP_AFTER == "C":
        _finalize(cx, nc, out)
        return nc, cx

    WOb, e_wo = load_w_contract_major(wo, D, D, "WOb")
    XT2q = cx.b16(16 * 1024, "XT2q")
    xqb = cx.b16(2048, "xqb", side="right")
    xD2 = [cx.f32(2048, "xDA"), cx.f32(2048, "xDB"), cx.f32(2048, "xDC")]
    ln2o2 = [cx.f32(2048, "ln2oA"), cx.f32(2048, "ln2oB"),
             cx.f32(2048, "ln2oC")]
    qscr = {
        "amax": cx.f32(64, "q_amax"), "pinv": cx.f32(64, "q_pinv"),
        "sc": cx.f32(64, "q_sc"),
        "s2": cx.b16(2048, "q_s2", side="right"), "s4": cx.b16(2048, "q_s4", side="right"),
        "t1": cx.b16(2048, "q_t1", side="right"), "zb": cx.b16(2048, "q_zb", side="right"),
        "sb16": cx.b16(64, "q_sb16", side="right"),
    }
    sy.waits(cx.guard("xDA", "xDB", "xDC"))
    v.waits(cx.guard("XT2q", "xqb", "q_s2", "q_s4", "q_t1", "q_zb", "q_sb16",
                     "ln2oA", "ln2oB", "ln2oC", "q_amax", "q_pinv", "q_sc"))
    a.waits(cx.guard("q_s2", "q_s4", "q_t1", "q_zb", "ln2oA", "ln2oB",
                     "ln2oC"))
    t.wait(e_wo)
    e_xload2 = [sy.dma(lambda e: e.dma_start(xD2[0], xown[0:128, :]), "xd"),
                None, None]
    qres = [None] * 8
    e_app_h = [None, None, None]
    e_y1st_h = [None, None, None]
    pend_tr = None
    for qc in range(8):
        par = qc % 3
        xD = xD2[par]
        ln2o = ln2o2[par]
        for hc in range(16):
            for ot in range(4):
                ev_mm = t.run(lambda e, qc=qc, hc=hc, ot=ot: e.matmul(
                    psa[:, ot * 512:(ot + 1) * 512],
                    OT[:, hc * 1024 + qc * 128: hc * 1024 + qc * 128 + 128],
                    WOb[:, hc * 2048 + ot * 512: hc * 2048 + (ot + 1) * 512],
                    start=(hc == 0), stop=(hc == 15)),
                    sync=(hc == 15 and ot == 3))
        v.wait(ev_mm)
        v.wait(e_xload2[par])
        e_y1 = v.run(lambda e, xD=xD: e.tensor_tensor(
            xD, xD, psa[:, :], ALU.add), sync=True)
        t.wait(e_y1)
        sy.wait(e_y1)
        e_y1st = sy.dma(lambda e, qc=qc, xD=xD: e.dma_start(
            y1d[qc * 128:(qc + 1) * 128, :], xD), "y1s")
        e_y1st_h[par] = e_y1st
        e_app = emit_ln(cx, xD, ln2o, 27 * par,
                        [e_y1] + ([qres[qc - 3]["yused"]] if qc >= 3 else []))
        e_app_h[par] = e_app
        if qc < 7:
            opar = (qc + 1) % 3
            sy.waits([e_app_h[opar], e_y1st_h[opar]])
            e_xload2[opar] = sy.dma(lambda e, qc=qc, opar=opar: e.dma_start(
                xD2[opar], xown[(qc + 1) * 128:(qc + 2) * 128, :]), "xd")
        if pend_tr is not None:
            tr_dst, e_qp = pend_tr
            emit_transpose(cx, xqb, tr_dst, 16, [e_qp],
                           pta if (qc - 1) % 2 == 0 else ptb)
            pend_tr = None
        res = {}
        qres[qc] = res
        deps = [e_app] + ([qres[qc - 1]["amax_rd"]] if qc >= 1 else [])
        e_q = emit_quant(cx, ln2o, xqb, 2048, qscr, "act", deps,
                         inplace_y=True, result=res, pool=False)
        pend_tr = (strided_dst(XT2q[:, qc * 128: qc * 128 + 128], 1024), e_q)
    tr_dst, e_qp = pend_tr
    emit_transpose(cx, xqb, tr_dst, 16, [e_qp], ptb)
    e_tD = (t.key, t.n)
    e_vD = (v.key, v.n)
    cx.taps["XT2q"] = (XT2q, [128, 16 * 1024])
    cx.rel16([e_tD], "WOb", "OT", "xqb")
    cx.rel16([e_vD], "q_s2", "q_s4", "q_t1", "q_zb", "q_sb16")
    cx.rel32([e_vD, e_y1st], "xDA", "xDB", "xDC", "ln2oA", "ln2oB", "ln2oC",
             "q_amax", "q_pinv", "q_sc")

    if STOP_AFTER == "D":
        _finalize(cx, nc, out)
        return nc, cx

    interT = cx.b16(64 * 512, "interT")
    inter16 = cx.b16(2048, "inter16", side="right")
    inter322 = [cx.f32(4 * 512, "inter32A"), cx.f32(4 * 512, "inter32B")]
    silo = cx.f32(1024, "silo")
    qscr2 = {
        "amax": cx.f32(64, "q_amax"), "pinv": cx.f32(64, "q_pinv"),
        "sc": cx.f32(64, "q_sc"),
        "s2": cx.b16(2048, "q_s2", side="right"), "s4": cx.b16(2048, "q_s4", side="right"),
        "t1": cx.b16(2048, "q_t1", side="right"), "zb": cx.b16(2048, "q_zb", side="right"),
        "sb16": cx.b16(64, "q_sb16", side="right"),
    }
    for q_ in (v, a):
        q_.waits(cx.guard("interT", "inter16", "inter32A", "inter32B", "silo",
                          "q_amax", "q_pinv", "q_sc",
                          "q_s2", "q_s4", "q_t1", "q_zb", "q_sb16"))

    wd0 = cx.b16(4096, "wd0")
    sy.waits(cx.guard("wd0") + [e_agd])
    e_wd0 = sy.dma(lambda e, wd0=wd0: e.dma_start(
        wd0, wdT_c[0, :, 0:4096]), "wd0")
    e_F_done = None
    for hq in range(2):
        qbase = hq * 512
        if e_F_done is not None:
            v.wait(e_F_done)
        if hq == 0:
            wblk = cx.b16(16384, "wblk", side="right")
            cx.wblk = wblk
            wgT2 = [wblk[:, 0:4096], wblk[:, 4096:8192]]
            wuT2 = [wblk[:, 8192:12288], wblk[:, 12288:16384]]
            sy.waits(cx.guard("wblk"))
            cx.wgT2, cx.wuT2 = wgT2, wuT2
            cx.e_wmm = [None, None]
            cx.e_wload = [None, None]
        wgT2, wuT2 = cx.wgT2, cx.wuT2
        if hq == 1:
            cx.e_wmm = [e_F_done, e_F_done]

        def load_gu(ig, hq):
            p = ig % 2
            sy.waits([cx.e_wmm[p], e_agg, e_agu])
            cx.e_wload[p] = [
                sy.dma(lambda e, ig=ig, p=p: e.dma_start(
                    wgT2[p], wgT_c[ig, :, :]), "crg%d" % p),
                sy.dma(lambda e, ig=ig, p=p: e.dma_start(
                    wuT2[p], wuT_c[ig, :, :]), "cru%d" % p)]

        e_psa_war = (v.key, v.n) if hq == 1 else None
        e_int_h = [None] * 4
        EPLAN = [2, 5, 2, 2, 2, 2, 2, 2]
        eq = {"gen": None, "pend": None, "pi": 0}

        def pump_eq(force=False):
            if eq["gen"] is None:
                return
            if force:
                n = 1 << 30
            else:
                n = EPLAN[eq["pi"]] if eq["pi"] < len(EPLAN) else 2
                eq["pi"] += 1
            while n > 0:
                try:
                    next(eq["gen"])
                    n -= 1
                except StopIteration:
                    eq["gen"] = None
                    return

        def flush_eq():
            pump_eq(force=True)
            if eq["pend"] is not None:
                trans_p, res_p = eq["pend"]
                for (tsrc, tdst, tn, tpsum) in trans_p:
                    emit_transpose(cx, tsrc, tdst, tn, [res_p["done"]], tpsum)
                eq["pend"] = None
        load_gu(0, hq)
        for ig in range(32):
            par_w = ig % 2
            wgT, wuT = wgT2[par_w], wuT2[par_w]
            e_wready = cx.e_wload[par_w]
            if ig + 1 < 32:
                load_gu(ig + 1, hq)
            t.waits(e_wready)
            if e_psa_war is not None:
                t.wait(e_psa_war)
            for qc in range(4):
                po = (qc % 2) * 512 + (qc // 2) * 256
                if qc >= 2:
                    t.wait(e_int_h[qc - 2])
                for hc in range(16):
                    ev_g = t.run(lambda e, qc=qc, hc=hc, qbase=qbase, wgT=wgT,
                                 po=po: e.matmul(
                        psa[:, po: po + 256],
                        XT2q[:, hc * 1024 + qbase + qc * 128:
                             hc * 1024 + qbase + qc * 128 + 128],
                        wgT[:, hc * 256:(hc + 1) * 256],
                        start=(hc == 0), stop=(hc == 15)),
                        sync=(hc == 15))
                    ev_u = t.run(lambda e, qc=qc, hc=hc, qbase=qbase, wuT=wuT,
                                 po=po: e.matmul(
                        psa[:, 1024 + po: 1024 + po + 256],
                        XT2q[:, hc * 1024 + qbase + qc * 128:
                             hc * 1024 + qbase + qc * 128 + 128],
                        wuT[:, hc * 256:(hc + 1) * 256],
                        start=(hc == 0), stop=(hc == 15)),
                        sync=(hc == 15))
                a.waits([ev_g, ev_u])
                e_sil = a.run(lambda e, qc=qc, po=po: e.activation(
                    silo[:, qc * 256:(qc + 1) * 256],
                    psa[:, po: po + 256], AF.Silu), sync=True)
                v.wait(e_sil)
                i32 = inter322[(ig // 2) % 2]
                e_int = v.run(lambda e, qc=qc, ig=ig, po=po, i32=i32:
                              e.tensor_tensor(
                    i32[:, qc * 512 + (ig % 2) * 256:
                        qc * 512 + (ig % 2) * 256 + 256],
                    silo[:, qc * 256:(qc + 1) * 256],
                    psa[:, 1024 + po: 1024 + po + 256], ALU.mult),
                    sync=True)
                e_int_h[qc] = e_int
                e_ep_last = e_int
                e_psa_war = e_int
                pump_eq()
            cx.e_wmm[par_w] = ev_u
            if ig % 2 == 1:
                flush_eq()
                ic0 = (ig // 2) * 4
                trans = [(inter16[:, qc * 512:(qc + 1) * 512],
                          strided_dst(interT[:, ic0 * 512 + qc * 128:
                                             ic0 * 512 + qc * 128 + 128], 512),
                          4, pta if (ig // 2) % 2 == 0 else ptb)
                         for qc in range(4)]
                res = {}
                eq["gen"] = quant_gen(cx, inter322[(ig // 2) % 2][:, :],
                                      inter16[:, :], 2048, qscr2, "act",
                                      [e_ep_last], res, inplace_y=True)
                eq["pend"] = (trans, res)
                eq["pi"] = 0
        flush_eq()
        if STOP_AFTER == "E" + str(hq):
            _finalize(cx, nc, out)
            return nc, cx
        wdT = cx.wblk
        e_wd_ready = None
        sy.waits([cx.e_wmm[0], cx.e_wmm[1]])
        e_add_hist = [None] * 4
        e_st_hist = [None] * 4
        e_add_last = None
        y1b = cx.f32(1024, "y1b")
        ob = cx.f32(1024, "ob")
        sy.waits(cx.guard("y1b", "ob"))
        v.waits(cx.guard("y1b", "ob"))
        e_crd4 = [None] * 4
        for c in range(1, 4):
            sy.waits([e_wd_ready, e_agd])
            e_crd4[c] = sy.dma(lambda e, c=c, wdT=wdT: e.dma_start(
                wdT[:, 4096 * c: 4096 * (c + 1)],
                wdT_c[c // 2, :,
                      (c % 2) * 4096:(c % 2) * 4096 + 4096]), "crd%d" % c)
        t.wait(e_psa_war)
        for ht in range(8):
            e_y1l4 = []
            for qc in range(4):
                sy.waits([e_add_hist[qc]])
                e_y1l4.append(sy.dma(
                    lambda e, qc=qc, ht=ht, qbase=qbase: e.dma_start(
                        y1b[:, qc * 256:(qc + 1) * 256],
                        y1d[qbase + qc * 128: qbase + (qc + 1) * 128,
                            ht * 256:(ht + 1) * 256]), "y1l%d" % qc))
            if e_add_last is not None:
                t.wait(e_add_last)
            for c in range(4):
                hot0 = (ht == 0 and c == 0)
                if hot0:
                    t.waits([e_wd0])
                    wsrc = wd0
                    woff = 0
                else:
                    t.wait(e_crd4[c])
                    wsrc = wdT
                    woff = 0 if hot0 else c * 4096
                for il in range(16):
                    ic = c * 16 + il
                    for qc in range(4):
                        ev_mm = t.run(lambda e, qc=qc, ic=ic, il=il, wsrc=wsrc,
                                      woff=woff: e.matmul(
                            psa[:, qc * 512: qc * 512 + 256],
                            interT[:, ic * 512 + qc * 128:
                                   ic * 512 + qc * 128 + 128],
                            wsrc[:, woff + il * 256: woff + (il + 1) * 256],
                            start=(ic == 0), stop=(ic == 63)),
                            sync=(il == 15 and qc == 3))
                if ht < 7:
                    sy.waits([ev_mm])
                    e_crd4[c] = sy.dma(lambda e, c=c, ht=ht, wdT=wdT:
                                       e.dma_start(
                        wdT[:, 4096 * c: 4096 * (c + 1)],
                        wdT_c[2 * (ht + 1) + c // 2, :,
                              (c % 2) * 4096:(c % 2) * 4096 + 4096]),
                        "crd%d" % c)
            for qc in range(4):
                v.waits([ev_mm, e_y1l4[qc], e_st_hist[qc]])
                e_add = v.run(lambda e, qc=qc, y1b=y1b, ob=ob: e.tensor_tensor(
                    ob[:, qc * 256:(qc + 1) * 256],
                    psa[:, qc * 512: qc * 512 + 256],
                    y1b[:, qc * 256:(qc + 1) * 256], ALU.add), sync=True)
                sy.wait(e_add)
                e_st = sy.dma(lambda e, qc=qc, ht=ht, qbase=qbase, ob=ob: e.dma_start(
                    out[qbase + qc * 128: qbase + (qc + 1) * 128,
                        ht * 256:(ht + 1) * 256], ob[:, qc * 256:(qc + 1) * 256]),
                    "ost%d" % qc)
                e_add_hist[qc] = e_add
                e_st_hist[qc] = e_st
                e_add_last = e_add
        e_F_done = (t.key, t.n)
        cx.rel32([e_add_last] + e_st_hist, "y1b", "ob")
        if hq == 1:
            cx.rel16([e_F_done], "wblk", "wd0")
        if STOP_AFTER == "F" + str(hq):
            _finalize(cx, nc, out)
            return nc, cx

    _finalize(cx, nc, out)
    return nc, cx


def _finalize(cx, nc, out):
    sy = cx.sy
    for name in TAPS:
        ap, shape = cx.taps[name]
        dten = nc.dram_tensor("tap_" + name, shape, ap.dtype, kind="ExternalOutput")
        for q in cx.queues:
            if q.n and q is not sy:
                sy.wait((q.key, q.n))
        sy.dma(lambda e, dten=dten, ap=ap, shape=shape: e.dma_start(
            dten[0:shape[0], :], ap), "tap")
    for q in cx.queues:
        if q.n and q is not sy:
            sy.wait((q.key, q.n))
    for skey, cnt in sorted(cx.slot_counts.items()):
        sy.wait((skey, cnt))

    import contextlib
    with contextlib.ExitStack() as stack:
        sems = {}
        for q in cx.queues:
            sems[q.key] = stack.enter_context(nc.semaphore("sem_" + q.key))
        for skey in sorted(cx.slot_counts):
            sems[skey] = stack.enter_context(nc.semaphore("sem_" + skey))
        block = stack.enter_context(nc.Block())

        @block.vector
        def _(e):
            cx.v.emit(e, sems)

        @block.scalar
        def _(e):
            cx.a.emit(e, sems)

        @block.tensor
        def _(e):
            cx.t.emit(e, sems)

        @block.gpsimd
        def _(e):
            cx.g.emit(e, sems)

        @block.sync
        def _(e):
            cx.sy.emit(e, sems)


def make_aux():
    inv_freq = (1.0 / 10000.0 ** (np.arange(0, PD, 2, dtype=np.float64) / PD))
    tpos = np.arange(S, dtype=np.float64)
    fr = tpos[None, :] * inv_freq[:, None]
    consts = np.zeros((128, 8), dtype=np.float32)
    consts[:, 0] = -2.0
    consts[:, 1] = -4.0
    consts[:, 2] = 1e-5
    return (np.cos(fr).astype(np.float32), np.sin(fr).astype(np.float32),
            np.eye(128, dtype=np.float32), consts)


def make_in_maps(inputs):
    x = np.ascontiguousarray(inputs["L0_attn_hidden_states"], dtype=np.float32)
    cos, sin, ident, consts = make_aux()
    in_maps = []
    for c in range(NCORES):
        b = c // 2
        h = c % 2
        in_maps.append({
            "xseq": np.ascontiguousarray(x[b]),
            "xown": np.ascontiguousarray(x[b, h * QLEN:(h + 1) * QLEN]),
            "rope_cos_q": np.ascontiguousarray(
                cos[:, h * QLEN:(h + 1) * QLEN] * np.float32(SCALE_ATTN)),
            "rope_sin_q": np.ascontiguousarray(
                sin[:, h * QLEN:(h + 1) * QLEN] * np.float32(SCALE_ATTN)),
            "W_DKV": inputs["L0_attn_W_DKV"], "W_DQ": inputs["L0_attn_W_DQ"],
            "W_UK": inputs["L0_attn_W_UK"], "W_UV": inputs["L0_attn_W_UV"],
            "W_UQ": inputs["L0_attn_W_UQ"], "W_KR": inputs["L0_attn_W_KR"],
            "W_QR": inputs["L0_attn_W_QR"], "W_O": inputs["L0_attn_W_O"],
            "wg_sl": np.ascontiguousarray(
                inputs["L0_ffn_w_gate"][c * (I // 8):(c + 1) * (I // 8)]),
            "wu_sl": np.ascontiguousarray(
                inputs["L0_ffn_w_up"][c * (I // 8):(c + 1) * (I // 8)]),
            "wd_sl": np.ascontiguousarray(
                inputs["L0_ffn_w_down"][c * (D // 8):(c + 1) * (D // 8)]),
            "rope_cos": cos, "rope_sin": sin, "identd": ident, "constsd": consts,
        })
    return in_maps


def kernel(**inputs):
    in_maps = make_in_maps(inputs)
    nc, cx = build_core(0)
    res = run_bass_kernel_spmd(nc, in_maps, core_ids=list(range(NCORES)))
    outs = np.empty((B, S, D), dtype=np.float32)
    for c in range(NCORES):
        b, h = c // 2, c % 2
        outs[b, h * QLEN:(h + 1) * QLEN, :] = res.results[c]["out"]
    return outs


# version 24
# speedup vs baseline: 1.0204x; 1.0204x over previous
import numpy as np
import concourse.bass as bass
import concourse.mybir as mybir
from concourse.bass_utils import run_bass_kernel_spmd

F32 = mybir.dt.float32
BF16 = mybir.dt.bfloat16
ALU = mybir.AluOpType
AF = mybir.ActivationFunctionType
AX = mybir.AxisListType

B, S, D = 4, 2048, 2048
H, HD = 16, 128
LAT, PD = 512, 64
I = 8192
NCORES = 8
QLEN = 1024
SCALE_ATTN = float(1.0 / np.sqrt(np.float32(HD + PD)))
MAGIC = float(np.float32(3 * 2 ** 22))
LN2C = float(np.log(2.0))
INV_LN2 = float(1.0 / np.log(2.0))

TAPS = []
STOP_AFTER = None


_SLOT_COUNTS = {}


class EQ:

    def __init__(self, key, slot_counts):
        self.key = key
        self.thunks = []
        self.n = 0
        self.waited = {}
        self.slot_counts = slot_counts

    def wait(self, ev):
        if ev is None:
            return
        key, val = ev
        if self.waited.get(key, 0) >= val:
            return
        self.waited[key] = val
        self.thunks.append(("w", key, val))

    def waits(self, evs):
        for e in evs:
            self.wait(e)

    def run(self, fn, sync=False):
        if sync:
            self.n += 1
            self.thunks.append(("i", fn, 1, self.key))
            return (self.key, self.n)
        self.thunks.append(("i", fn, 0, self.key))
        return None

    def dma(self, fn, slot):
        skey = "d_" + slot
        prev = self.slot_counts.get(skey, 0)
        if prev:
            self.wait((skey, prev))
        self.slot_counts[skey] = prev + 16
        self.thunks.append(("i", fn, 16, skey))
        return (skey, self.slot_counts[skey])

    def coll(self, fn, slot):
        skey = "d_" + slot
        prev = self.slot_counts.get(skey, 0)
        if prev:
            self.wait((skey, prev))
        self.slot_counts[skey] = prev + 1
        self.thunks.append(("i", fn, 1, skey))
        return (skey, self.slot_counts[skey])

    def emit(self, eng, sems):
        for t in self.thunks:
            if t[0] == "w":
                eng.wait_ge(sems[t[1]], t[2])
            else:
                ins = t[1](eng)
                if t[2]:
                    ins.then_inc(sems[t[3]], t[2])


class Arena:

    def __init__(self, tensor, total, name):
        self.t = tensor
        self.free = [(0, total, [])]
        self.live = {}
        self.name = name
        self.peak = 0

    def alloc(self, n, tag, align=32, side="left"):
        assert tag not in self.live, tag
        order = list(range(len(self.free)))
        if side == "right":
            order.reverse()
        for i in order:
            off, sz, evs = self.free[i]
            if side == "left":
                aoff = (off + align - 1) // align * align
            else:
                aoff = (off + sz - n) // align * align
                if aoff < off:
                    continue
            if aoff + n <= off + sz:
                repl = [p for p in ((off, aoff - off, list(evs)),
                                    (aoff + n, off + sz - aoff - n, list(evs)))
                        if p[1] > 0]
                self.free[i:i + 1] = repl
                self.live[tag] = (aoff, n)
                used = sum(v[1] for v in self.live.values())
                self.peak = max(self.peak, used)
                return aoff, list(evs)
        raise MemoryError(f"{self.name} OOM for {tag} ({n}); free="
                          f"{[(o, s) for o, s, _ in self.free]} "
                          f"live={ {k: v[1] for k, v in self.live.items()} }")

    def release(self, tag, evs):
        off, n = self.live.pop(tag)
        self.free.append((off, n, list(evs)))
        self.free.sort(key=lambda p: p[0])
        merged = []
        for off, n, evs in self.free:
            if merged and merged[-1][0] + merged[-1][1] == off:
                merged[-1][1] += n
                merged[-1][2] = merged[-1][2] + evs
            else:
                merged.append([off, n, list(evs)])
        self.free = [(o, s, e) for o, s, e in merged]


AR16_N = 80384
AR32_N = 12608


class Ctx:
    def __init__(self, nc, core_id):
        self.nc = nc
        self.core = core_id
        self.half = core_id % 2
        self.q0 = self.half * QLEN
        self.slot_counts = {}
        self.v = EQ("v", self.slot_counts)
        self.a = EQ("a", self.slot_counts)
        self.t = EQ("t", self.slot_counts)
        self.g = EQ("g", self.slot_counts)
        self.sy = EQ("sy", self.slot_counts)
        self.queues = [self.v, self.a, self.t, self.g, self.sy]
        self.ar16_t = nc.alloc_sbuf_tensor("arena16", [128, AR16_N], BF16)
        self.ar32_t = nc.alloc_sbuf_tensor("arena32", [128, AR32_N], F32)
        self.ar16 = Arena(self.ar16_t, AR16_N, "ar16")
        self.ar32 = Arena(self.ar32_t, AR32_N, "ar32")
        self.taps = {}
        self.guards = {}

    def b16(self, n, tag, rows=128, side="left"):
        off, evs = self.ar16.alloc(n, tag, side=side)
        self.guards[tag] = evs
        return self.ar16_t[0:rows, off:off + n]

    def f32(self, n, tag, rows=128):
        off, evs = self.ar32.alloc(n, tag)
        self.guards[tag] = evs
        return self.ar32_t[0:rows, off:off + n]

    def guard(self, *tags):
        out = []
        for tg in tags:
            out.extend(self.guards.get(tg, []))
        return out

    def rel16(self, evs, *tags):
        for tg in tags:
            self.ar16.release(tg, evs)

    def rel32(self, evs, *tags):
        for tg in tags:
            self.ar32.release(tg, evs)


def emit_transpose(cx, src_ap, dst_fn, ntiles, dep_evs, psum):
    t, v = cx.t, cx.v
    t.waits(dep_evs)
    batch = psum.shape[1] // 128
    ev_ev = None
    j = 0
    while j < ntiles:
        k = min(batch, ntiles - j)
        ev = None
        for i in range(k):
            ev = t.run(lambda e, j=j, i=i: e.transpose(
                psum[:, i * 128:(i + 1) * 128],
                src_ap[:, (j + i) * 128:(j + i + 1) * 128], cx.identb),
                sync=(i == k - 1))
        v.wait(ev)
        ev_ev = v.run(lambda e, j=j, k=k: e.tensor_copy(
            dst_fn(j, k), psum[:, 0:k * 128].rearrange("p (k c) -> p k c", c=128)),
            sync=True)
        t.wait(ev_ev)
        j += k
    return ev_ev


def strided_dst(base_ap, tile_stride):
    def f(j, k):
        t = base_ap.tensor
        part = list(base_ap.ap[0])
        return bass.AP(t, base_ap.offset + j * tile_stride,
                       [part, [tile_stride, k], [1, 128]])
    return f


def quant_gen(cx, src_ap, out_ap, W, scr, kind, dep_evs, result,
              transposes=None, inplace_y=False, pool=False):
    v, a = cx.v, cx.a
    nb = W // 32
    amax = scr["amax"][:, :nb]
    pinv = scr["pinv"][:, :nb]
    sc = scr["sc"][:, :nb]
    sb16 = scr["sb16"][:, :nb]
    y = src_ap if inplace_y else scr["y"][:, :W]
    s2 = scr["s2"][:, :W]
    s4 = scr["s4"][:, :W]
    t1 = scr["t1"][:, :W]
    zb = scr["zb"][:, :W]

    v.waits(dep_evs)
    a.waits(dep_evs)
    e_amax = v.run(lambda e: e.tensor_reduce(
        amax.unsqueeze(2), src_ap.rearrange("p (b k) -> p b k", k=32),
        AX.X, ALU.max, apply_absolute_value=True), sync=True)
    yield
    v.wait(e_amax)
    e1 = v.run(lambda e: e.tensor_scalar(amax, amax, 2.0 ** -30, None, ALU.max),
               sync=True)
    yield
    a.wait(e1)
    e2 = a.run(lambda e: e.activation(pinv, amax, AF.Ln), sync=True)
    result["amax_rd"] = e2
    yield
    v.wait(e2)
    e3 = v.run(lambda e: e.tensor_scalar(pinv, pinv, INV_LN2, -0.49609375,
                                         ALU.mult, ALU.add), sync=True)
    yield
    v.wait(e3)
    e4 = v.run(lambda e: e.tensor_scalar(pinv, pinv, MAGIC, MAGIC,
                                         ALU.add, ALU.subtract), sync=True)
    yield
    v.wait(e4)
    e5 = v.run(lambda e: e.tensor_scalar(sc, pinv, LN2C, -2.0 * LN2C,
                                         ALU.mult, ALU.add), sync=True)
    e6 = v.run(lambda e: e.tensor_scalar(pinv, pinv, -LN2C, 2.0 * LN2C,
                                         ALU.mult, ALU.add), sync=True)
    yield
    a.waits([e5, e6])
    e7 = a.run(lambda e: e.activation(sc, sc, AF.Exp), sync=True)
    e8 = a.run(lambda e: e.activation(pinv, pinv, AF.Exp), sync=True)
    yield
    v.waits([e7, e8])
    e9 = v.run(lambda e: e.tensor_copy(sb16, sc), sync=True)
    yield
    e10 = v.run(lambda e: e.tensor_tensor(
        y.rearrange("p (b k) -> p b k", k=32),
        src_ap.rearrange("p (b k) -> p b k", k=32),
        pinv.unsqueeze(2).broadcast_to([128, nb, 32]), ALU.mult), sync=True)
    if not inplace_y:
        result["yused"] = e10
    yield
    a.wait(e10)
    ep2 = a.run(lambda e: e.activation(s2, y, AF.Sign, bias=cx.cneg2), sync=True)
    en2 = a.run(lambda e: e.activation(t1, y, AF.Sign, scale=-1.0,
                                       bias=cx.cneg2), sync=True)
    ep4 = a.run(lambda e: e.activation(s4, y, AF.Sign, bias=cx.cneg4), sync=True)
    en4 = a.run(lambda e: e.activation(zb, y, AF.Sign, scale=-1.0,
                                       bias=cx.cneg4), sync=True)
    yield
    gq = cx.g if pool else v
    gq.waits([ep2, en2, ep4, en4])
    ga = gq.run(lambda e: e.tensor_tensor(s2, s2, t1, ALU.add), sync=True)
    gb = gq.run(lambda e: e.tensor_tensor(s4, s4, zb, ALU.add), sync=True)
    yield
    v.waits([ga, gb])
    if kind == "act":
        e11 = v.run(lambda e: e.scalar_tensor_tensor(
            t1, s2, -2.0, s4, ALU.mult, ALU.subtract), sync=True)
        cadd, cm2, cm0, cm4, clip = 2.0, 0.0625, 0.5, 0.125, 7.5
    else:
        e4q = v.run(lambda e: e.tensor_scalar(t1, s4, 0.25, None, ALU.mult),
                    sync=True)
        yield
        v.wait(e4q)
        e11 = v.run(lambda e: e.scalar_tensor_tensor(
            t1, s2, -0.5, t1, ALU.mult, ALU.subtract), sync=True)
        cadd, cm2, cm0, cm4, clip = 0.5, 0.25, 2.0, 0.5, 6.0
    yield
    v.wait(e11)
    e12 = v.run(lambda e: e.scalar_tensor_tensor(
        y, t1, cadd, y, ALU.add, ALU.mult), sync=True)
    yield
    v.wait(e12)
    e13 = v.run(lambda e: e.tensor_scalar(zb, y, MAGIC, MAGIC,
                                          ALU.add, ALU.subtract), sync=True)
    if inplace_y:
        result["yused"] = e13
    e14 = v.run(lambda e: e.tensor_scalar(t1, s2, cm2, cm0, ALU.mult, ALU.add),
                sync=True)
    yield
    v.waits([e13, e14])
    e15 = v.run(lambda e: e.scalar_tensor_tensor(
        t1, s4, cm4, t1, ALU.mult, ALU.add), sync=True)
    yield
    gq.waits([e15] + ([e13] if pool else []))
    e16 = gq.run(lambda e: e.tensor_tensor(zb, zb, t1, ALU.mult), sync=True)
    yield
    v.wait(e16)
    e17 = v.run(lambda e: e.tensor_scalar(zb, zb, clip, -clip,
                                          ALU.min, ALU.max), sync=True)
    yield
    v.waits([e17, e9])
    e18 = v.run(lambda e: e.tensor_tensor(
        out_ap.rearrange("p (b k) -> p b k", k=32),
        zb.rearrange("p (b k) -> p b k", k=32),
        sb16.unsqueeze(2).broadcast_to([128, nb, 32]), ALU.mult), sync=True)
    result["done"] = e18
    if transposes:
        for (tsrc, tdst, tn, tpsum) in transposes:
            result["tdone"] = emit_transpose(cx, tsrc, tdst, tn, [e18], tpsum)


class QuantPipe:

    def __init__(self, depth=2):
        self.depth = depth
        self.active = []

    def push(self, gen):
        self.active.append(gen)
        while len(self.active) >= self.depth:
            self.step_until_done_one()

    def step_until_done_one(self):
        while self.active:
            for g in list(self.active):
                try:
                    next(g)
                except StopIteration:
                    self.active.remove(g)
                    return

    def flush(self):
        while self.active:
            self.step_until_done_one()


def emit_rope(cx, psrc, dst, cosap, sinap, W, ev_dep):
    v = cx.v
    r1 = cx.f32(W, "ropeA", rows=32)
    r2 = cx.f32(W, "ropeB", rows=32)
    v.waits(cx.guard("ropeA", "ropeB"))
    v.waits(ev_dep if isinstance(ev_dep, list) else [ev_dep])
    lo, hi = psrc[0:32, :], psrc[32:64, :]
    ev = v.run(lambda e: e.tensor_tensor(r1, lo, cosap, ALU.mult), sync=True)
    ev = v.run(lambda e: e.tensor_tensor(r2, hi, sinap, ALU.mult), sync=True)
    v.wait(ev)
    ev = v.run(lambda e: e.tensor_tensor(dst[0:32, :], r1, r2, ALU.subtract),
               sync=True)
    v.wait(ev)
    ev = v.run(lambda e: e.tensor_tensor(r1, hi, cosap, ALU.mult), sync=True)
    ev = v.run(lambda e: e.tensor_tensor(r2, lo, sinap, ALU.mult), sync=True)
    v.wait(ev)
    ev = v.run(lambda e: e.tensor_tensor(dst[32:64, :], r1, r2, ALU.add),
               sync=True)
    cx.rel32([ev], "ropeA", "ropeB")
    return ev


def emit_ln(cx, buf, out_ap, base, dep_evs):
    v, a = cx.v, cx.a
    s24 = cx.statst[:, base:base + 24]
    mean = cx.statst[:, base + 24:base + 25]
    var = cx.statst[:, base + 25:base + 26]
    rstd = cx.statst[:, base + 26:base + 27]
    v.waits(dep_evs)
    ev = None
    for j in range(4):
        ev = v.run(lambda e, j=j: e.bn_stats(
            s24[:, 6 * j:6 * j + 6], buf[:, 512 * j:512 * (j + 1)]),
            sync=(j == 3))
    v.wait(ev)
    e2 = v.run(lambda e: e.bn_aggr(cx.statst[:, base + 24:base + 26], s24),
               sync=True)
    a.wait(e2)
    e3 = a.run(lambda e: e.activation(rstd, var, AF.Ln, bias=cx.ceps), sync=True)
    a.wait(e3)
    e4 = a.run(lambda e: e.activation(rstd, rstd, AF.Exp, scale=-0.5), sync=True)
    v.wait(e4)
    e5 = v.run(lambda e: e.tensor_scalar(out_ap, buf, mean, rstd,
                                         ALU.subtract, ALU.mult), sync=True)
    return e5


def emit_quant(cx, src_ap, out_ap, W, scr, kind, dep_evs, inplace_y=False,
               transposes=None, result=None, pool=False):
    if result is None:
        result = {}
    for _ in quant_gen(cx, src_ap, out_ap, W, scr, kind, dep_evs, result,
                       transposes=transposes, inplace_y=inplace_y, pool=pool):
        pass
    return result["tdone"] if transposes else result["done"]


def build_core(core_id):
    nc = bass.Bass("TRN2", target_bir_lowering=False)
    cx = Ctx(nc, core_id)
    v, a, t, g, sy = cx.v, cx.a, cx.t, cx.g, cx.sy
    q0 = cx.q0

    xseq = nc.dram_tensor("xseq", [S, D], F32, kind="ExternalInput")
    xown = nc.dram_tensor("xown", [QLEN, D], F32, kind="ExternalInput")
    wdkv = nc.dram_tensor("W_DKV", [D, LAT], F32, kind="ExternalInput")
    wdq = nc.dram_tensor("W_DQ", [D, LAT], F32, kind="ExternalInput")
    wuk = nc.dram_tensor("W_UK", [LAT, D], F32, kind="ExternalInput")
    wuv = nc.dram_tensor("W_UV", [LAT, D], F32, kind="ExternalInput")
    wuq = nc.dram_tensor("W_UQ", [LAT, D], F32, kind="ExternalInput")
    wkr = nc.dram_tensor("W_KR", [D, PD], F32, kind="ExternalInput")
    wqr = nc.dram_tensor("W_QR", [LAT, H * PD], F32, kind="ExternalInput")
    wo = nc.dram_tensor("W_O", [D, D], F32, kind="ExternalInput")
    wg_sl = nc.dram_tensor("wg_sl", [I // 8, D], F32, kind="ExternalInput")
    wu_sl = nc.dram_tensor("wu_sl", [I // 8, D], F32, kind="ExternalInput")
    wd_sl = nc.dram_tensor("wd_sl", [D // 8, I], F32, kind="ExternalInput")
    ropec = nc.dram_tensor("rope_cos", [32, S], F32, kind="ExternalInput")
    ropes = nc.dram_tensor("rope_sin", [32, S], F32, kind="ExternalInput")
    ropecq = nc.dram_tensor("rope_cos_q", [32, QLEN], F32, kind="ExternalInput")
    ropesq = nc.dram_tensor("rope_sin_q", [32, QLEN], F32, kind="ExternalInput")
    identd = nc.dram_tensor("identd", [128, 128], F32, kind="ExternalInput")
    constsd = nc.dram_tensor("constsd", [128, 8], F32, kind="ExternalInput")
    out = nc.dram_tensor("out", [QLEN, D], F32, kind="ExternalOutput")
    y1d = nc.dram_tensor("y1scr", [QLEN, D], F32, kind="Internal")
    wgT_c = nc.dram_tensor("wgT_c", [32, 128, 4096], BF16, kind="Internal")
    wuT_c = nc.dram_tensor("wuT_c", [32, 128, 4096], BF16, kind="Internal")
    wdT_c = nc.dram_tensor("wdT_c", [16, 128, 8192], BF16, kind="Internal")
    cg_c = nc.dram_tensor("cg_c", [4, 128, 4096], BF16, kind="Internal")
    cu_c = nc.dram_tensor("cu_c", [4, 128, 4096], BF16, kind="Internal")
    cd_c = nc.dram_tensor("cd_c", [2, 128, 8192], BF16, kind="Internal")

    consts = nc.alloc_sbuf_tensor("consts", [128, 8], F32)
    identf = nc.alloc_sbuf_tensor("identf", [128, 128], F32)
    identb = nc.alloc_sbuf_tensor("identb", [128, 128], BF16)
    stats = nc.alloc_sbuf_tensor("stats", [128, 84], F32)
    cx.statst = stats
    cx.identb = identb[:, :]
    cx.cneg2 = consts[:, 0:1]
    cx.cneg4 = consts[:, 1:2]
    cx.ceps = consts[:, 2:3]
    nc.const_aps.register(F32, -2.0, consts[:, 0:1])
    nc.const_aps.register(F32, -4.0, consts[:, 1:2])
    nc.const_aps.register(F32, 1e-5, consts[:, 2:3])

    psa = nc.alloc_psum_tensor("psa", [128, 2048], F32)
    psb = nc.alloc_psum_tensor("psb", [128, 1024], F32)
    pta = nc.alloc_psum_tensor("pta", [128, 1024], BF16)
    ptb = nc.alloc_psum_tensor("ptb", [128, 1024], BF16)

    e_consts = sy.dma(lambda e: e.dma_start(consts[:, :], constsd[:, :]), "ini0")
    e_identf = sy.dma(lambda e: e.dma_start(identf[:, :], identd[:, :]), "ini1")
    cx.e_cosq, cx.e_sinq = None, None
    v.wait(e_identf)
    e_identb = v.run(lambda e: e.tensor_copy(identb[:, :], identf[:, :]), sync=True)
    t.wait(e_identb)
    a.wait(e_consts)
    v.wait(e_consts)

    csb = cx.f32(S, "cos", rows=64)
    cosst = csb[0:32, :]
    sinst = bass.AP(csb.tensor, csb.offset + 32 * AR32_N, list(csb.ap)) \
        if False else None
    sy.waits(cx.guard("cos"))
    e_cos = sy.dma(lambda e: e.dma_start(cosst, ropec[:, :]), "ini2")
    sinst = cx.ar32_t[32:64, cx.ar32.live["cos"][0]:cx.ar32.live["cos"][0] + S]
    e_sin = sy.dma(lambda e: e.dma_start(sinst, ropes[:, :]), "ini3")
    cx.e_cos, cx.e_sin = e_cos, e_sin
    CKVT = cx.b16(4 * 2048, "CKVT")
    KRT = cx.b16(2048, "KRT", rows=64)

    def load_w_contract_major(dram, rows, cols, tag):
        nk = rows // 128
        bf = cx.b16(nk * cols, tag)
        g.waits(cx.guard(tag))
        ev = g.dma(lambda e, bf=bf, dram=dram, cols=cols: e.dma_start(
            bf.rearrange("p (c l) -> p c l", l=cols),
            dram[:, :].rearrange("(c p) l -> p c l", p=128)), "wl_" + tag)
        return bf, ev

    WDKVb, e_wdkv = load_w_contract_major(wdkv, D, LAT, "WDKVb")
    WKRb, e_wkr = load_w_contract_major(wkr, D, PD, "WKRb")
    e_ckvt_ev = [None] * 4
    e_cqt_ev = [None] * 2
    e_krt_done = [None]

    def emit_B_ckvt_tt(tt):
        t.waits([e_wdkv] + [e_xt_tc[4 * tt + i] for i in range(4)])
        for lc in range(4):
            t.wait(e_ckvt_ev[lc])
            ev = None
            for kc in range(16):
                ev = t.run(lambda e, lc=lc, kc=kc, tt=tt: e.matmul(
                    psa[:, lc * 512:(lc + 1) * 512],
                    WDKVb[:, kc * LAT + lc * 128: kc * LAT + (lc + 1) * 128],
                    XTr[tt % 2][:, kc * 512:(kc + 1) * 512],
                    start=(kc == 0), stop=(kc == 15)),
                    sync=(kc == 15))
            a.wait(ev)
            e_ev = a.run(lambda e, lc=lc, tt=tt: e.activation(
                CKVT[:, lc * 2048 + tt * 512: lc * 2048 + (tt + 1) * 512],
                psa[:, lc * 512:(lc + 1) * 512], AF.Copy), sync=True)
            e_ckvt_ev[lc] = e_ev

    def emit_B_kr_tt(tt):
        t.waits([e_wkr] + [e_xt_tc[4 * tt + i] for i in range(4)])
        t.wait(e_krt_done[0])
        ev = None
        for kc in range(16):
            ev = t.run(lambda e, kc=kc, tt=tt: e.matmul(
                psb[0:64, 0:512],
                WKRb[:, kc * PD:(kc + 1) * PD],
                XTr[tt % 2][:, kc * 512:(kc + 1) * 512],
                start=(kc == 0), stop=(kc == 15)),
                sync=(kc == 15))
        e_krt_done[0] = emit_rope(cx, psb[0:64, 0:512],
                                  KRT[:, tt * 512:(tt + 1) * 512],
                                  cosst[0:32, tt * 512:(tt + 1) * 512],
                                  sinst[0:32, tt * 512:(tt + 1) * 512], 512,
                                  [ev, cx.e_cos, cx.e_sin])

    def emit_B_cqt_tt(qt):
        t.waits([e_wdq] + [e_xtq_tc[4 * qt + i] for i in range(4)])
        t.wait(e_krt_done[0])
        for lc2 in range(2):
            for lc in (2 * lc2, 2 * lc2 + 1):
                t.wait(e_cqt_ev[lc % 2])
                ev = None
                for kc in range(16):
                    ev = t.run(lambda e, lc=lc, kc=kc, qt=qt: e.matmul(
                        psb[:, (lc % 2) * 512:((lc % 2) + 1) * 512],
                        WDQb[:, kc * LAT + lc * 128: kc * LAT + (lc + 1) * 128],
                        XTQ[:, kc * 1024 + qt * 512: kc * 1024 + (qt + 1) * 512],
                        start=(kc == 0), stop=(kc == 15)),
                        sync=(kc == 15))
                a.wait(ev)
                e_ev = a.run(lambda e, lc=lc, qt=qt: e.activation(
                    CQT[:, lc * 1024 + qt * 512: lc * 1024 + (qt + 1) * 512],
                    psb[:, (lc % 2) * 512:((lc % 2) + 1) * 512], AF.Copy),
                    sync=True)
                e_cqt_ev[lc % 2] = e_ev

    XTr = [cx.b16(16 * 512, "XTr0"), cx.b16(16 * 512, "XTr1")]
    e_xt_tc = [None] * 16
    e_xtq_tc = [None] * 8
    xch = [cx.f32(2048, "xA"), cx.f32(2048, "xB")]
    lnout = cx.b16(2048, "lnout")
    e_load = [None, None]
    e_app_prev = None
    e_sq_last = None
    sy.waits(cx.guard("xA", "xB"))

    e_load[0] = sy.dma(lambda e: e.dma_start(xch[0], xseq[0:128, :]), "xb0")
    e_load[1] = sy.dma(lambda e: e.dma_start(xch[1], xseq[128:256, :]), "xb1")
    for tc in range(16):
        buf = xch[tc % 2]
        par = tc % 2
        deps = [e_load[par]]
        if e_app_prev is not None:
            deps.append(e_app_prev)
        e_app = emit_ln(cx, buf, lnout, 27 * par, deps)
        e_app_prev = e_app
        if tc + 2 < 16:
            sy.wait(e_app)
            e_load[par] = sy.dma(lambda e, tc=tc, buf=buf: e.dma_start(
                buf, xseq[(tc + 2) * 128:(tc + 3) * 128, :]), "xb%d" % par)
        e_xt_tc[tc] = emit_transpose(
            cx, lnout,
            strided_dst(XTr[(tc // 4) % 2][:, (tc % 4) * 128:(tc % 4) * 128 + 128],
                        512),
            16, [e_app], pta if par == 0 else ptb)
        v.wait((cx.t.key, cx.t.n))
        if tc % 4 == 3:
            emit_B_ckvt_tt(tc // 4)
            emit_B_kr_tt(tc // 4)
    XTQ = cx.b16(16 * 1024, "XTQ")
    CQT = cx.b16(4 * 1024, "CQT")
    WDQb, e_wdq = load_w_contract_major(wdq, D, LAT, "WDQb")
    v.waits(cx.guard("XTQ"))
    sy.waits([e_app_prev])
    e_load[0] = sy.dma(lambda e: e.dma_start(xch[0], xown[0:128, :]), "xb0")
    e_load[1] = sy.dma(lambda e: e.dma_start(xch[1], xown[128:256, :]), "xb1")
    for tc in range(8):
        buf = xch[tc % 2]
        par = tc % 2
        e_app = emit_ln(cx, buf, lnout, 27 * par, [e_load[par], e_app_prev])
        e_app_prev = e_app
        if tc + 2 < 8:
            sy.waits([e_app])
            e_load[par] = sy.dma(lambda e, tc=tc, buf=buf: e.dma_start(
                buf, xown[(tc + 2) * 128:(tc + 3) * 128, :]), "xb%d" % par)
        e_xtq_tc[tc] = emit_transpose(
            cx, lnout, strided_dst(XTQ[:, tc * 128: tc * 128 + 128], 1024),
            16, [e_app], pta if par == 0 else ptb)
        v.wait((cx.t.key, cx.t.n))
        if tc % 4 == 3:
            emit_B_cqt_tt(tc // 4)

    e_XT = (cx.v.key, cx.v.n)
    e_trans_last = (cx.t.key, cx.t.n)

    cx.rel32([e_app_prev], "xA", "xB")
    cx.rel16([(cx.t.key, cx.t.n)], "lnout", "WDKVb", "WDQb", "WKRb", "XTQ",
             "XTr0", "XTr1")
    if STOP_AFTER == "B":
        _finalize(cx, nc, out)
        return nc, cx


    raww2 = [cx.f32(1024, "rawwA"), cx.f32(1024, "rawwB")]
    wbuf2 = [cx.b16(1024, "wbufA"), cx.b16(1024, "wbufB")]
    scrW = {
        "amax": cx.f32(32, "qw_amax"), "pinv": cx.f32(32, "qw_pinv"),
        "sc": cx.f32(32, "qw_sc"),
        "s2": cx.b16(1024, "qw_s2"), "s4": cx.b16(1024, "qw_s4"),
        "t1": cx.b16(1024, "qw_t1"), "zb": cx.b16(1024, "qw_zb"),
        "sb16": cx.b16(32, "qw_sb16"), "wqb": cx.b16(1024, "qw_wqb"),
    }
    sy.waits(cx.guard("rawwA", "rawwB"))
    for q_ in (v, a):
        q_.waits(cx.guard("wbufA", "wbufB", "qw_s2", "qw_s4", "qw_t1", "qw_zb",
                          "qw_sb16", "qw_wqb", "qw_amax", "qw_pinv", "qw_sc"))

    wjob_defs = []
    for igl in range(4):
        for ch in range(2):
            for cc in range(2):
                wjob_defs.append((wg_sl[igl * 256 + ch * 128:
                                        igl * 256 + (ch + 1) * 128,
                                        cc * 1024:(cc + 1) * 1024],
                                  cg_c, igl, cc * 2048 + ch * 128))
    for igl in range(4):
        for ch in range(2):
            for cc in range(2):
                wjob_defs.append((wu_sl[igl * 256 + ch * 128:
                                        igl * 256 + (ch + 1) * 128,
                                        cc * 1024:(cc + 1) * 1024],
                                  cu_c, igl, cc * 2048 + ch * 128))
    for ch in range(2):
        for cc in range(8):
            wjob_defs.append((wd_sl[ch * 128:(ch + 1) * 128,
                                    cc * 1024:(cc + 1) * 1024],
                              cd_c, cc // 4, ((cc % 4) * 8) * 256 + ch * 128))
    wres = [None] * len(wjob_defs)
    e_wld = [None] * len(wjob_defs)
    e_cw = [None, None]
    gath = {}
    gather_at = {15: ("agg", cg_c, wgT_c), 31: ("agu", cu_c, wuT_c),
                 47: ("agd", cd_c, wdT_c)}

    def wjob_load(k):
        dslice = wjob_defs[k][0]
        par = k % 2
        if k >= 2:
            sy.waits([wres[k - 2].get("yused")])
        e_wld[k] = sy.dma(lambda e, dslice=dslice, par=par: e.dma_start(
            raww2[par], dslice), "wld" + ("A" if par == 0 else "B"))

    def wjob_gen(k):
        if k + 1 < len(wjob_defs):
            wjob_load(k + 1)
        _, cten, idx, coff = wjob_defs[k]
        par = k % 2
        res = {}
        wres[k] = res
        trans = [(scrW["wqb"][:, 0:1024],
                  strided_dst(wbuf2[par][:, 0:128], 128), 8,
                  pta if par == 0 else ptb)]
        deps = [e_wld[k], e_cw[par]]
        for _x in quant_gen(cx, raww2[par], scrW["wqb"][:, 0:1024], 1024,
                            scrW, "w", deps, res, transposes=trans,
                            inplace_y=True):
            yield
        g.waits([res["tdone"]])
        e_cw[par] = g.dma(
            lambda e, cten=cten, idx=idx, coff=coff, par=par: e.dma_start(
                bass.AP(cten, idx * 128 * cten.shape[2] + coff,
                        [[cten.shape[2], 128], [256, 8], [1, 128]]),
                wbuf2[par][:, 0:1024].rearrange("p (j c) -> p j c", c=128)),
            "cw" + ("A" if par == 0 else "B"))
        if k in gather_at:
            slot, cten_g, gten = gather_at[k]
            g.waits([e_cw[0], e_cw[1]])
            gath[slot] = g.coll(
                lambda e, cten_g=cten_g, gten=gten: e.collective_compute(
                    "AllGather", ALU.bypass,
                    replica_groups=[list(range(NCORES))],
                    ins=[cten_g.ap().opt()], outs=[gten.ap().opt()]), slot)

    wpipe = {"cur": None, "next": 0}

    def pump_w(nsteps):
        while nsteps > 0:
            if wpipe["cur"] is None:
                k = wpipe["next"]
                if k >= len(wjob_defs):
                    return
                if k == 0:
                    wjob_load(0)
                wpipe["next"] = k + 1
                wpipe["cur"] = wjob_gen(k)
            try:
                next(wpipe["cur"])
                nsteps -= 1
            except StopIteration:
                wpipe["cur"] = None

    def finish_wjobs():
        pump_w(1 << 30)
        ev1, ev2, ev3 = gath["agg"], gath["agu"], gath["agd"]
        e_done = (v.key, v.n)
        cx.rel32([wres[-1].get("yused"), wres[-2].get("yused")],
                 "rawwA", "rawwB")
        cx.rel32([e_done], "qw_amax", "qw_pinv", "qw_sc")
        cx.rel16([e_cw[0], e_cw[1]], "wbufA", "wbufB")
        cx.rel16([e_done], "qw_s2", "qw_s4", "qw_t1", "qw_zb", "qw_sb16",
                 "qw_wqb")
        return ev1, ev2, ev3

    t.waits(e_ckvt_ev + e_cqt_ev + [e_krt_done[0]])
    cosq = cx.f32(QLEN, "cosq", rows=32)
    sinq = cx.ar32_t[32:64, cx.ar32.live["cosq"][0]:cx.ar32.live["cosq"][0] + QLEN]
    sy.waits(cx.guard("cosq"))
    e_cosq = sy.dma(lambda e: e.dma_start(cosq, ropecq[:, :]), "ini4")
    e_sinq = sy.dma(lambda e: e.dma_start(sinq, ropesq[:, :]), "ini5")
    cx.e_cosq, cx.e_sinq = e_cosq, e_sinq
    WUQg = [cx.b16(4 * 256, "WUQg0"), cx.b16(4 * 256, "WUQg1")]
    WQRg = [cx.b16(4 * 128, "WQRg0"), cx.b16(4 * 128, "WQRg1")]
    WUKg = [cx.b16(4 * 256, "WUKg0"), cx.b16(4 * 256, "WUKg1")]
    WUVg = [cx.b16(4 * 256, "WUVg0"), cx.b16(4 * 256, "WUVg1")]
    g.waits(cx.guard("WUQg0", "WUQg1", "WQRg0", "WQRg1", "WUKg0", "WUKg1",
                     "WUVg0", "WUVg1"))
    e_gw = [None, None]
    e_genmm = [None, None]

    def load_group_w(grp):
        par = grp % 2
        g.waits([e_genmm[par]])
        evs = []
        for (dst, wsrc, colw) in ((WUQg[par], wuq, 128), (WQRg[par], wqr, 64),
                                  (WUKg[par], wuk, 128), (WUVg[par], wuv, 128)):
            evs.append(g.dma(lambda e, dst=dst, wsrc=wsrc, colw=colw, grp=grp:
                             e.dma_start(
                dst.rearrange("p (c l) -> p c l", l=2 * colw),
                wsrc[:, grp * 2 * colw:(grp + 1) * 2 * colw]
                .rearrange("(c p) l -> p c l", p=128)), "gw%d" % par))
        e_gw[par] = evs

    OT = cx.b16(16 * 1024, "OT")
    QCg = cx.b16(2 * 1024, "QCg")
    QRTg = cx.b16(2 * 1024, "QRTg", rows=64)
    KCg = cx.b16(2 * 2048, "KCg")
    Vg = cx.b16(16 * 256, "Vg")
    Pb2 = [cx.b16(2048, "Pb0"), cx.b16(2048, "Pb1")]
    PT4 = cx.b16(16 * 256, "PT4")

    v.waits(cx.guard("OT", "QCg", "QRTg", "KCg", "Vg", "Pb0", "Pb1", "PT4"))
    a.waits(cx.guard("Pb0", "Pb1", "OT", "QCg", "KCg", "Vg"))
    e_exp_hist = [None, None]
    e_ps_hist = [None, None]
    e_pt_hist = [None, None]
    e_exptt_prev = [None] * 4
    e_pv = None
    load_group_w(0)
    for grp in range(8):
        h0 = 2 * grp
        par_g = grp % 2
        if grp + 1 < 8:
            load_group_w(grp + 1)
        t.waits(e_gw[par_g])

        def emit_qc(hh):
            ev_mm = None
            for kc in range(4):
                for qt in range(2):
                    ev_mm = t.run(lambda e, hh=hh, kc=kc, qt=qt,
                                  par_g=par_g: e.matmul(
                        psb[:, qt * 512:(qt + 1) * 512],
                        WUQg[par_g][:, kc * 256 + hh * 128:
                                    kc * 256 + hh * 128 + 128],
                        CQT[:, kc * 1024 + qt * 512: kc * 1024 + (qt + 1) * 512],
                        start=(kc == 0), stop=(kc == 3)),
                        sync=(kc == 3 and qt == 1))
            a.wait(ev_mm)
            return a.run(lambda e, hh=hh: e.activation(
                QCg[:, hh * 1024:(hh + 1) * 1024], psb[:, :], AF.Copy,
                scale=SCALE_ATTN), sync=True)

        def emit_kc(hh):
            ev_mm = None
            for kc in range(4):
                for tt in range(4):
                    ev_mm = t.run(lambda e, hh=hh, kc=kc, tt=tt,
                                  par_g=par_g: e.matmul(
                        psa[:, tt * 512:(tt + 1) * 512],
                        WUKg[par_g][:, kc * 256 + hh * 128:
                                    kc * 256 + hh * 128 + 128],
                        CKVT[:, kc * 2048 + tt * 512: kc * 2048 + (tt + 1) * 512],
                        start=(kc == 0), stop=(kc == 3)),
                        sync=(kc == 3 and tt == 3))
            a.wait(ev_mm)
            return a.run(lambda e, hh=hh: e.activation(
                KCg[:, hh * 2048:(hh + 1) * 2048], psa[:, :], AF.Copy),
                sync=True)

        def emit_qrt(hh):
            ev_mm = None
            for kc in range(4):
                for qt in range(2):
                    ev_mm = t.run(lambda e, hh=hh, kc=kc, qt=qt,
                                  par_g=par_g: e.matmul(
                        psb[0:64, qt * 512:(qt + 1) * 512],
                        WQRg[par_g][:, kc * 128 + hh * 64:
                                    kc * 128 + hh * 64 + 64],
                        CQT[:, kc * 1024 + qt * 512: kc * 1024 + (qt + 1) * 512],
                        start=(kc == 0), stop=(kc == 3)),
                        sync=(kc == 3 and qt == 1))
            e_rope = emit_rope(cx, psb[0:64, 0:1024],
                               QRTg[:, hh * 1024:(hh + 1) * 1024],
                               cosq, sinq, 1024,
                               [ev_mm, cx.e_cosq, cx.e_sinq])
            return ev_mm, e_rope

        e_qc0 = emit_qc(0)
        e_kc0 = emit_kc(0)
        pump_w(12)
        t.wait(e_qc0)
        e_qc1 = emit_qc(1)
        t.wait(e_kc0)
        e_kc1 = emit_kc(1)
        pump_w(12)
        t.wait(e_qc1)
        for tcg in range(4):
            ev_mm = None
            for tc4 in range(4):
                tc = tcg * 4 + tc4
                for lc in range(4):
                    ev_mm = t.run(lambda e, tc=tc, tc4=tc4, lc=lc,
                                  par_g=par_g: e.matmul(
                        psb[:, tc4 * 256:(tc4 + 1) * 256],
                        CKVT[:, lc * 2048 + tc * 128: lc * 2048 + tc * 128 + 128],
                        WUVg[par_g][:, lc * 256: lc * 256 + 256],
                        start=(lc == 0), stop=(lc == 3)),
                        sync=(tc4 == 3 and lc == 3))
            a.wait(ev_mm)
            e_vg = a.run(lambda e, tcg=tcg: e.activation(
                Vg[:, tcg * 1024:(tcg + 1) * 1024], psb[:, :], AF.Copy), sync=True)
            t.wait(e_vg)
        ev_qr0, e_rope0 = emit_qrt(0)
        t.wait(e_rope0)
        ev_qr1, e_rope1 = emit_qrt(1)
        e_genmm[par_g] = ev_qr1
        t.wait(e_kc1)
        pump_w(8)
        pend_pt = None
        for u in range(17):
            if True:
                hh = min(u // 8, 1)
                h = h0 + hh
                qc = u % 8
                if u < 16:
                    par = qc % 2
                    pm = stats[:, 54 + 8 * par: 58 + 8 * par]
                    vmax = stats[:, 58 + 8 * par: 59 + 8 * par]
                    nm = stats[:, 59 + 8 * par: 60 + 8 * par]
                    ssum = stats[:, 60 + 8 * par: 61 + 8 * par]
                    rr = stats[:, 61 + 8 * par: 62 + 8 * par]
                    Pb = Pb2[par]
                    ev_tt = []
                    for tt in range(4):
                        t.wait(e_exptt_prev[tt])
                        t.run(lambda e, hh=hh, qc=qc, tt=tt, h0=h0: e.matmul(
                            psa[:, tt * 512:(tt + 1) * 512],
                            QCg[:, hh * 1024 + qc * 128: hh * 1024 + qc * 128 + 128],
                            KCg[:, hh * 2048 + tt * 512: hh * 2048 + (tt + 1) * 512],
                            start=True, stop=False))
                    for tt in range(4):
                        ev_tt.append(t.run(lambda e, hh=hh, qc=qc, tt=tt, h0=h0: e.matmul(
                            psa[:, tt * 512:(tt + 1) * 512],
                            QRTg[:, hh * 1024 + qc * 128: hh * 1024 + qc * 128 + 128],
                            KRT[:, tt * 512:(tt + 1) * 512],
                            start=False, stop=True), sync=True))
                    v.wait(e_exp_hist[par])
                    for tt in range(4):
                        v.wait(ev_tt[tt])
                        v.run(lambda e, tt=tt, pm=pm: e.tensor_reduce(
                            pm[:, tt:tt + 1], psa[:, tt * 512:(tt + 1) * 512],
                            AX.X, ALU.max), sync=(tt == 3))
                    e_pm = (v.key, v.n)
                    a.wait(e_pt_hist[par])
                    v.wait(e_pm)
                    e_mx = v.run(lambda e, pm=pm, vmax=vmax: e.tensor_reduce(
                        vmax, pm, AX.X, ALU.max, negate=True), sync=True)
                    a.wait(e_mx)
                    for tt in range(4):
                        e_exptt_prev[tt] = a.run(
                            lambda e, tt=tt, vmax=vmax, pm=pm, Pb=Pb: e.activation(
                                Pb[:, tt * 512:(tt + 1) * 512],
                                psa[:, tt * 512:(tt + 1) * 512], AF.Exp,
                                bias=vmax,
                                accum_out=pm[:, tt:tt + 1]), sync=True)
                    e_exp = e_exptt_prev[3]
                    e_exp_hist[par] = e_exp
                if pend_pt is not None:
                    phh, pqc = pend_pt
                    ph = h0 + phh
                    if pqc % 2 == 0 and e_pv is not None:
                        v.wait(e_pv)
                    pp = pqc % 2
                    emit_transpose(
                        cx, Pb2[pp][:, 0:1024],
                        strided_dst(PT4[:, pp * 128: pp * 128 + 128], 256),
                        8, [e_ps_hist[pp]], pta)
                    emit_transpose(
                        cx, Pb2[pp][:, 1024:2048],
                        strided_dst(PT4[:, pp * 128 + 2048: pp * 128 + 2176],
                                    256),
                        8, [e_ps_hist[pp]], ptb)
                    e_pt_hist[pp] = (t.key, t.n)
                    if pqc % 2 == 1:
                        if pqc == 1 and phh == 0:
                            t.wait(e_rope1)
                        ev_mm = None
                        for tc in range(16):
                            ev_mm = t.run(lambda e, hh=phh, tc=tc, h0=h0: e.matmul(
                                psb[:, 0:256],
                                Vg[:, tc * 256 + hh * 128: tc * 256 + hh * 128 + 128],
                                PT4[:, tc * 256:(tc + 1) * 256],
                                start=(tc == 0), stop=(tc == 15)),
                                sync=(tc == 15))
                        e_pv = ev_mm
                        a.wait(ev_mm)
                        e_ot = a.run(lambda e, h=ph, pqc=pqc: e.activation(
                            OT[:, h * 1024 + (pqc // 2) * 256:
                               h * 1024 + (pqc // 2) * 256 + 256],
                            psb[:, 0:256], AF.Copy), sync=True)
                        t.wait(e_ot)
                    pend_pt = None
                if u == 16:
                    break
                pump_w(4)
                v.wait(e_exp)
                e_sm = v.run(lambda e, pm=pm, ssum=ssum: e.tensor_reduce(
                    ssum, pm, AX.X, ALU.add), sync=True)
                v.wait(e_sm)
                e_rr = v.run(lambda e, ssum=ssum, rr=rr: e.reciprocal(rr, ssum),
                             sync=True)
                v.wait(e_rr)
                e_ps = v.run(lambda e, rr=rr, Pb=Pb: e.tensor_scalar(
                    Pb, Pb, rr, None, ALU.mult), sync=True)
                e_ps_hist[par] = e_ps
                pend_pt = (hh, qc)
    e_agg, e_agu, e_agd = finish_wjobs()
    cx.taps["OT"] = (OT, [128, 16 * 1024])
    e_tC = (t.key, t.n)
    cx.rel16([e_tC], "QCg", "QRTg", "KCg", "Vg", "Pb0", "Pb1", "PT4",
             "WUQg0", "WUQg1", "WQRg0", "WQRg1", "WUKg0", "WUKg1",
             "WUVg0", "WUVg1", "CKVT", "CQT", "KRT")
    cx.rel32([e_tC], "cos", "cosq")

    if STOP_AFTER == "C":
        _finalize(cx, nc, out)
        return nc, cx

    WOb, e_wo = load_w_contract_major(wo, D, D, "WOb")
    XT2q = cx.b16(16 * 1024, "XT2q")
    xqb = cx.b16(2048, "xqb", side="right")
    xD2 = [cx.f32(2048, "xDA"), cx.f32(2048, "xDB"), cx.f32(2048, "xDC")]
    ln2o2 = [cx.f32(2048, "ln2oA"), cx.f32(2048, "ln2oB"),
             cx.f32(2048, "ln2oC")]
    qscr = {
        "amax": cx.f32(64, "q_amax"), "pinv": cx.f32(64, "q_pinv"),
        "sc": cx.f32(64, "q_sc"),
        "s2": cx.b16(2048, "q_s2", side="right"), "s4": cx.b16(2048, "q_s4", side="right"),
        "t1": cx.b16(2048, "q_t1", side="right"), "zb": cx.b16(2048, "q_zb", side="right"),
        "sb16": cx.b16(64, "q_sb16", side="right"),
    }
    sy.waits(cx.guard("xDA", "xDB", "xDC"))
    v.waits(cx.guard("XT2q", "xqb", "q_s2", "q_s4", "q_t1", "q_zb", "q_sb16",
                     "ln2oA", "ln2oB", "ln2oC", "q_amax", "q_pinv", "q_sc"))
    a.waits(cx.guard("q_s2", "q_s4", "q_t1", "q_zb", "ln2oA", "ln2oB",
                     "ln2oC"))
    t.wait(e_wo)
    e_xload2 = [sy.dma(lambda e: e.dma_start(xD2[0], xown[0:128, :]), "xd"),
                None, None]
    qres = [None] * 8
    e_app_h = [None, None, None]
    e_y1st_h = [None, None, None]
    pend_tr = None
    for qc in range(8):
        par = qc % 3
        xD = xD2[par]
        ln2o = ln2o2[par]
        for hc in range(16):
            for ot in range(4):
                ev_mm = t.run(lambda e, qc=qc, hc=hc, ot=ot: e.matmul(
                    psa[:, ot * 512:(ot + 1) * 512],
                    OT[:, hc * 1024 + qc * 128: hc * 1024 + qc * 128 + 128],
                    WOb[:, hc * 2048 + ot * 512: hc * 2048 + (ot + 1) * 512],
                    start=(hc == 0), stop=(hc == 15)),
                    sync=(hc == 15 and ot == 3))
        v.wait(ev_mm)
        v.wait(e_xload2[par])
        e_y1 = v.run(lambda e, xD=xD: e.tensor_tensor(
            xD, xD, psa[:, :], ALU.add), sync=True)
        t.wait(e_y1)
        sy.wait(e_y1)
        e_y1st = sy.dma(lambda e, qc=qc, xD=xD: e.dma_start(
            y1d[qc * 128:(qc + 1) * 128, :], xD), "y1s")
        e_y1st_h[par] = e_y1st
        e_app = emit_ln(cx, xD, ln2o, 27 * par,
                        [e_y1] + ([qres[qc - 3]["yused"]] if qc >= 3 else []))
        e_app_h[par] = e_app
        if qc < 7:
            opar = (qc + 1) % 3
            sy.waits([e_app_h[opar], e_y1st_h[opar]])
            e_xload2[opar] = sy.dma(lambda e, qc=qc, opar=opar: e.dma_start(
                xD2[opar], xown[(qc + 1) * 128:(qc + 2) * 128, :]), "xd")
        if pend_tr is not None:
            tr_dst, e_qp = pend_tr
            emit_transpose(cx, xqb, tr_dst, 16, [e_qp],
                           pta if (qc - 1) % 2 == 0 else ptb)
            pend_tr = None
        res = {}
        qres[qc] = res
        deps = [e_app] + ([qres[qc - 1]["amax_rd"]] if qc >= 1 else [])
        e_q = emit_quant(cx, ln2o, xqb, 2048, qscr, "act", deps,
                         inplace_y=True, result=res, pool=False)
        pend_tr = (strided_dst(XT2q[:, qc * 128: qc * 128 + 128], 1024), e_q)
    tr_dst, e_qp = pend_tr
    emit_transpose(cx, xqb, tr_dst, 16, [e_qp], ptb)
    e_tD = (t.key, t.n)
    e_vD = (v.key, v.n)
    cx.taps["XT2q"] = (XT2q, [128, 16 * 1024])
    cx.rel16([e_tD], "WOb", "OT", "xqb")
    cx.rel16([e_vD], "q_s2", "q_s4", "q_t1", "q_zb", "q_sb16")
    cx.rel32([e_vD, e_y1st], "xDA", "xDB", "xDC", "ln2oA", "ln2oB", "ln2oC",
             "q_amax", "q_pinv", "q_sc")

    if STOP_AFTER == "D":
        _finalize(cx, nc, out)
        return nc, cx

    interT = cx.b16(64 * 512, "interT")
    inter16 = cx.b16(2048, "inter16", side="right")
    inter322 = [cx.f32(4 * 512, "inter32A"), cx.f32(4 * 512, "inter32B")]
    silo = cx.f32(1024, "silo")
    qscr2 = {
        "amax": cx.f32(64, "q_amax"), "pinv": cx.f32(64, "q_pinv"),
        "sc": cx.f32(64, "q_sc"),
        "s2": cx.b16(2048, "q_s2", side="right"), "s4": cx.b16(2048, "q_s4", side="right"),
        "t1": cx.b16(2048, "q_t1", side="right"), "zb": cx.b16(2048, "q_zb", side="right"),
        "sb16": cx.b16(64, "q_sb16", side="right"),
    }
    for q_ in (v, a):
        q_.waits(cx.guard("interT", "inter16", "inter32A", "inter32B", "silo",
                          "q_amax", "q_pinv", "q_sc",
                          "q_s2", "q_s4", "q_t1", "q_zb", "q_sb16"))

    wd0 = cx.b16(4096, "wd0")
    sy.waits(cx.guard("wd0") + [e_agd])
    e_wd0 = sy.dma(lambda e, wd0=wd0: e.dma_start(
        wd0, wdT_c[0, :, 0:4096]), "wd0")
    e_F_done = None
    for hq in range(2):
        qbase = hq * 512
        if e_F_done is not None:
            v.wait(e_F_done)
        if hq == 0:
            wblk = cx.b16(16384, "wblk", side="right")
            cx.wblk = wblk
            wgT2 = [wblk[:, 0:4096], wblk[:, 4096:8192]]
            wuT2 = [wblk[:, 8192:12288], wblk[:, 12288:16384]]
            sy.waits(cx.guard("wblk"))
            cx.wgT2, cx.wuT2 = wgT2, wuT2
            cx.e_wmm = [None, None]
            cx.e_wload = [None, None]
        wgT2, wuT2 = cx.wgT2, cx.wuT2
        if hq == 1:
            cx.e_wmm = [e_F_done, e_F_done]

        def load_gu(ig, hq):
            p = ig % 2
            sy.waits([cx.e_wmm[p], e_agg, e_agu])
            cx.e_wload[p] = [
                sy.dma(lambda e, ig=ig, p=p: e.dma_start(
                    wgT2[p], wgT_c[ig, :, :]), "crg%d" % p),
                sy.dma(lambda e, ig=ig, p=p: e.dma_start(
                    wuT2[p], wuT_c[ig, :, :]), "cru%d" % p)]

        e_psa_war = (v.key, v.n) if hq == 1 else None
        e_int_h = [None] * 4
        EPLAN = [2, 5, 2, 2, 2, 2, 2, 2]
        eq = {"gen": None, "pend": None, "pi": 0}

        def pump_eq(force=False):
            if eq["gen"] is None:
                return
            if force:
                n = 1 << 30
            else:
                n = EPLAN[eq["pi"]] if eq["pi"] < len(EPLAN) else 2
                eq["pi"] += 1
            while n > 0:
                try:
                    next(eq["gen"])
                    n -= 1
                except StopIteration:
                    eq["gen"] = None
                    return

        def flush_eq():
            pump_eq(force=True)
            if eq["pend"] is not None:
                trans_p, res_p = eq["pend"]
                for (tsrc, tdst, tn, tpsum) in trans_p:
                    emit_transpose(cx, tsrc, tdst, tn, [res_p["done"]], tpsum)
                eq["pend"] = None
        load_gu(0, hq)
        for ig in range(32):
            par_w = ig % 2
            wgT, wuT = wgT2[par_w], wuT2[par_w]
            e_wready = cx.e_wload[par_w]
            if ig + 1 < 32:
                load_gu(ig + 1, hq)
            t.waits(e_wready)
            if e_psa_war is not None:
                t.wait(e_psa_war)
            for qc in range(4):
                po = (qc % 2) * 512 + (qc // 2) * 256
                if qc >= 2:
                    t.wait(e_int_h[qc - 2])
                for hc in range(16):
                    ev_g = t.run(lambda e, qc=qc, hc=hc, qbase=qbase, wgT=wgT,
                                 po=po: e.matmul(
                        psa[:, po: po + 256],
                        XT2q[:, hc * 1024 + qbase + qc * 128:
                             hc * 1024 + qbase + qc * 128 + 128],
                        wgT[:, hc * 256:(hc + 1) * 256],
                        start=(hc == 0), stop=(hc == 15)),
                        sync=(hc == 15))
                    ev_u = t.run(lambda e, qc=qc, hc=hc, qbase=qbase, wuT=wuT,
                                 po=po: e.matmul(
                        psa[:, 1024 + po: 1024 + po + 256],
                        XT2q[:, hc * 1024 + qbase + qc * 128:
                             hc * 1024 + qbase + qc * 128 + 128],
                        wuT[:, hc * 256:(hc + 1) * 256],
                        start=(hc == 0), stop=(hc == 15)),
                        sync=(hc == 15))
                a.waits([ev_g, ev_u])
                e_sil = a.run(lambda e, qc=qc, po=po: e.activation(
                    silo[:, qc * 256:(qc + 1) * 256],
                    psa[:, po: po + 256], AF.Silu), sync=True)
                v.wait(e_sil)
                i32 = inter322[(ig // 2) % 2]
                e_int = v.run(lambda e, qc=qc, ig=ig, po=po, i32=i32:
                              e.tensor_tensor(
                    i32[:, qc * 512 + (ig % 2) * 256:
                        qc * 512 + (ig % 2) * 256 + 256],
                    silo[:, qc * 256:(qc + 1) * 256],
                    psa[:, 1024 + po: 1024 + po + 256], ALU.mult),
                    sync=True)
                e_int_h[qc] = e_int
                e_ep_last = e_int
                e_psa_war = e_int
                pump_eq()
            cx.e_wmm[par_w] = ev_u
            if ig % 2 == 1:
                flush_eq()
                ic0 = (ig // 2) * 4
                trans = [(inter16[:, qc * 512:(qc + 1) * 512],
                          strided_dst(interT[:, ic0 * 512 + qc * 128:
                                             ic0 * 512 + qc * 128 + 128], 512),
                          4, pta if (ig // 2) % 2 == 0 else ptb)
                         for qc in range(4)]
                res = {}
                eq["gen"] = quant_gen(cx, inter322[(ig // 2) % 2][:, :],
                                      inter16[:, :], 2048, qscr2, "act",
                                      [e_ep_last], res, inplace_y=True)
                eq["pend"] = (trans, res)
                eq["pi"] = 0
        flush_eq()
        if STOP_AFTER == "E" + str(hq):
            _finalize(cx, nc, out)
            return nc, cx
        wdT = cx.wblk
        e_wd_ready = None
        sy.waits([cx.e_wmm[0], cx.e_wmm[1]])
        e_add_hist = [None] * 4
        e_st_hist = [None] * 4
        e_add_last = None
        y1b = cx.f32(1024, "y1b")
        ob = cx.f32(1024, "ob")
        sy.waits(cx.guard("y1b", "ob"))
        v.waits(cx.guard("y1b", "ob"))
        e_crd4 = [None] * 4
        for c in range(1, 4):
            sy.waits([e_wd_ready, e_agd])
            e_crd4[c] = sy.dma(lambda e, c=c, wdT=wdT: e.dma_start(
                wdT[:, 4096 * c: 4096 * (c + 1)],
                wdT_c[c // 2, :,
                      (c % 2) * 4096:(c % 2) * 4096 + 4096]), "crd%d" % c)
        t.wait(e_psa_war)
        for ht in range(8):
            e_y1l4 = []
            for qc in range(4):
                sy.waits([e_add_hist[qc]])
                e_y1l4.append(sy.dma(
                    lambda e, qc=qc, ht=ht, qbase=qbase: e.dma_start(
                        y1b[:, qc * 256:(qc + 1) * 256],
                        y1d[qbase + qc * 128: qbase + (qc + 1) * 128,
                            ht * 256:(ht + 1) * 256]), "y1l%d" % qc))
            if e_add_last is not None:
                t.wait(e_add_last)
            for c in range(4):
                hot0 = (ht == 0 and c == 0)
                if hot0:
                    t.waits([e_wd0])
                    wsrc = wd0
                    woff = 0
                else:
                    t.wait(e_crd4[c])
                    wsrc = wdT
                    woff = 0 if hot0 else c * 4096
                for il in range(16):
                    ic = c * 16 + il
                    for qc in range(4):
                        ev_mm = t.run(lambda e, qc=qc, ic=ic, il=il, wsrc=wsrc,
                                      woff=woff: e.matmul(
                            psa[:, qc * 512: qc * 512 + 256],
                            interT[:, ic * 512 + qc * 128:
                                   ic * 512 + qc * 128 + 128],
                            wsrc[:, woff + il * 256: woff + (il + 1) * 256],
                            start=(ic == 0), stop=(ic == 63)),
                            sync=(il == 15 and qc == 3))
                if ht < 7:
                    sy.waits([ev_mm])
                    e_crd4[c] = sy.dma(lambda e, c=c, ht=ht, wdT=wdT:
                                       e.dma_start(
                        wdT[:, 4096 * c: 4096 * (c + 1)],
                        wdT_c[2 * (ht + 1) + c // 2, :,
                              (c % 2) * 4096:(c % 2) * 4096 + 4096]),
                        "crd%d" % c)
            for qc in range(4):
                v.waits([ev_mm, e_y1l4[qc], e_st_hist[qc]])
                e_add = v.run(lambda e, qc=qc, y1b=y1b, ob=ob: e.tensor_tensor(
                    ob[:, qc * 256:(qc + 1) * 256],
                    psa[:, qc * 512: qc * 512 + 256],
                    y1b[:, qc * 256:(qc + 1) * 256], ALU.add), sync=True)
                sy.wait(e_add)
                e_st = sy.dma(lambda e, qc=qc, ht=ht, qbase=qbase, ob=ob: e.dma_start(
                    out[qbase + qc * 128: qbase + (qc + 1) * 128,
                        ht * 256:(ht + 1) * 256], ob[:, qc * 256:(qc + 1) * 256]),
                    "ost%d" % qc)
                e_add_hist[qc] = e_add
                e_st_hist[qc] = e_st
                e_add_last = e_add
        e_F_done = (t.key, t.n)
        cx.rel32([e_add_last] + e_st_hist, "y1b", "ob")
        if hq == 1:
            cx.rel16([e_F_done], "wblk", "wd0")
        if STOP_AFTER == "F" + str(hq):
            _finalize(cx, nc, out)
            return nc, cx

    _finalize(cx, nc, out)
    return nc, cx


def _finalize(cx, nc, out):
    sy = cx.sy
    for name in TAPS:
        ap, shape = cx.taps[name]
        dten = nc.dram_tensor("tap_" + name, shape, ap.dtype, kind="ExternalOutput")
        for q in cx.queues:
            if q.n and q is not sy:
                sy.wait((q.key, q.n))
        sy.dma(lambda e, dten=dten, ap=ap, shape=shape: e.dma_start(
            dten[0:shape[0], :], ap), "tap")
    for q in cx.queues:
        if q.n and q is not sy:
            sy.wait((q.key, q.n))
    for skey, cnt in sorted(cx.slot_counts.items()):
        sy.wait((skey, cnt))

    import contextlib
    with contextlib.ExitStack() as stack:
        sems = {}
        for q in cx.queues:
            sems[q.key] = stack.enter_context(nc.semaphore("sem_" + q.key))
        for skey in sorted(cx.slot_counts):
            sems[skey] = stack.enter_context(nc.semaphore("sem_" + skey))
        block = stack.enter_context(nc.Block())

        @block.vector
        def _(e):
            cx.v.emit(e, sems)

        @block.scalar
        def _(e):
            cx.a.emit(e, sems)

        @block.tensor
        def _(e):
            cx.t.emit(e, sems)

        @block.gpsimd
        def _(e):
            cx.g.emit(e, sems)

        @block.sync
        def _(e):
            cx.sy.emit(e, sems)


def make_aux():
    inv_freq = (1.0 / 10000.0 ** (np.arange(0, PD, 2, dtype=np.float64) / PD))
    tpos = np.arange(S, dtype=np.float64)
    fr = tpos[None, :] * inv_freq[:, None]
    consts = np.zeros((128, 8), dtype=np.float32)
    consts[:, 0] = -2.0
    consts[:, 1] = -4.0
    consts[:, 2] = 1e-5
    return (np.cos(fr).astype(np.float32), np.sin(fr).astype(np.float32),
            np.eye(128, dtype=np.float32), consts)


def make_in_maps(inputs):
    x = np.ascontiguousarray(inputs["L0_attn_hidden_states"], dtype=np.float32)
    cos, sin, ident, consts = make_aux()
    in_maps = []
    for c in range(NCORES):
        b = c // 2
        h = c % 2
        in_maps.append({
            "xseq": np.ascontiguousarray(x[b]),
            "xown": np.ascontiguousarray(x[b, h * QLEN:(h + 1) * QLEN]),
            "rope_cos_q": np.ascontiguousarray(
                cos[:, h * QLEN:(h + 1) * QLEN] * np.float32(SCALE_ATTN)),
            "rope_sin_q": np.ascontiguousarray(
                sin[:, h * QLEN:(h + 1) * QLEN] * np.float32(SCALE_ATTN)),
            "W_DKV": inputs["L0_attn_W_DKV"], "W_DQ": inputs["L0_attn_W_DQ"],
            "W_UK": inputs["L0_attn_W_UK"], "W_UV": inputs["L0_attn_W_UV"],
            "W_UQ": inputs["L0_attn_W_UQ"], "W_KR": inputs["L0_attn_W_KR"],
            "W_QR": inputs["L0_attn_W_QR"], "W_O": inputs["L0_attn_W_O"],
            "wg_sl": np.ascontiguousarray(
                inputs["L0_ffn_w_gate"][c * (I // 8):(c + 1) * (I // 8)]),
            "wu_sl": np.ascontiguousarray(
                inputs["L0_ffn_w_up"][c * (I // 8):(c + 1) * (I // 8)]),
            "wd_sl": np.ascontiguousarray(
                inputs["L0_ffn_w_down"][c * (D // 8):(c + 1) * (D // 8)]),
            "rope_cos": cos, "rope_sin": sin, "identd": ident, "constsd": consts,
        })
    return in_maps


def kernel(**inputs):
    in_maps = make_in_maps(inputs)
    nc, cx = build_core(0)
    res = run_bass_kernel_spmd(nc, in_maps, core_ids=list(range(NCORES)))
    outs = np.empty((B, S, D), dtype=np.float32)
    for c in range(NCORES):
        b, h = c // 2, c % 2
        outs[b, h * QLEN:(h + 1) * QLEN, :] = res.results[c]["out"]
    return outs
